# version 25
# baseline (speedup 1.0000x reference)
import numpy as np
import concourse.bass as bass
import concourse.mybir as mybir
from concourse.bass_utils import run_bass_kernel_spmd

F32 = mybir.dt.float32
BF16 = mybir.dt.bfloat16
AF = mybir.ActivationFunctionType
ALU = mybir.AluOpType
AX = mybir.AxisListType.X

D = 1024
NFC = D // 128
L = 256
NLC = L // 128
NE = 8
NJJ = 16
HIDSLOTS = 8
USE_CC = False
SCALE = 1.0 / 32.0
LN_EPS = 1e-5
ROUTER_EPS = 1e-8
CLAMP = 10000.0


class Builder:
    ENGS = ("sync", "gpsimd", "tensor", "vector", "scalar")
    OWN = {"tensor": "sT", "vector": "sV", "scalar": "sS"}

    def __init__(self, nc):
        self.nc = nc
        self.ops = {e: [] for e in self.ENGS}
        self.sems = {}
        self.cnt = {}
        self.last_wait = {}
        self.bank_free = {}
        self.buf_free = {}

    def sem(self, name, ctxs):
        h = self.nc.semaphore(name)
        ctxs.append(h)
        self.sems[name] = h.__enter__()
        self.cnt[name] = 0

    def op(self, eng, fn):
        self.ops[eng].append(fn)

    def wait(self, eng, sem, val):
        if val is None or val <= 0:
            return
        assert val <= self.cnt[sem], f"wait {eng} {sem}>={val} not yet emitted"
        key = (eng, sem)
        if self.last_wait.get(key, -1) >= val:
            return
        self.last_wait[key] = val
        h = self.sems[sem]
        self.op(eng, lambda e, h=h, v=val: e.wait_ge(h, v))

    def wpair(self, eng, pair):
        if pair is not None:
            self.wait(eng, pair[0], pair[1])

    def inc(self, sem, n=1):
        self.cnt[sem] += n
        return self.cnt[sem]

    def chain(self, eng, fns):
        sem = self.OWN[eng]
        h = self.sems[sem]
        last = None
        for f in fns:
            if last is not None:
                self.wait(eng, sem, last)
            self.op(eng, lambda e, f=f, h=h: f(e).then_inc(h, 1))
            last = self.inc(sem)
        return last

    def vbatch(self, fns):
        h = self.sems["sV"]
        for f in fns[:-1]:
            self.op("vector", f)
        self.op("vector", lambda v, f=fns[-1], h=h: f(v).then_inc(h, 1))
        return self.inc("sV")

    def op_inc(self, eng, fn):
        sem = self.OWN[eng]
        h = self.sems[sem]
        self.op(eng, lambda e, f=fn, h=h: f(e).then_inc(h, 1))
        return self.inc(sem)

    def emit_into(self, block):
        for eng in self.ENGS:
            if not self.ops[eng]:
                continue
            fns = self.ops[eng]
            getattr(block, eng)(lambda e, fns=fns: [f(e) for f in fns])
        self.ops = {e: [] for e in self.ENGS}


def build_program(T=2048, TB=8192, GRP=512, use_cc=USE_CC):
    if use_cc:
        TB = T
    NTC = T // 128
    TH = T // 2
    NTH = TH // 128
    SL = min(512, TH)
    NSL = TH // SL
    NGT = GRP // 128
    NGR = TB // GRP
    NOG = T // GRP
    GSL = min(512, GRP)
    NGS = GRP // GSL

    nc = bass.Bass("TRN2", target_bir_lowering=False)
    b = Builder(nc)
    ctxs = []

    def sb(name, shape, dt, clist):
        h = nc.sbuf_tensor(name, shape, dt)
        clist.append(h)
        return h.__enter__()

    xtok = nc.dram_tensor("xtok", [NTC, 128, D], F32, kind="ExternalInput")
    xT_d = nc.dram_tensor("xT", [NFC, 128, TB], F32, kind="ExternalInput")
    wth_d = nc.dram_tensor("wth", [NFC, 128, L], F32, kind="ExternalInput")
    wph_d = nc.dram_tensor("wph", [NFC, 128, L], F32, kind="ExternalInput")
    wg_d = nc.dram_tensor("wg", [NFC, 128, D], F32, kind="ExternalInput")
    rw_d = nc.dram_tensor("rw", [NFC, 128, NE], F32, kind="ExternalInput")
    gam_d = nc.dram_tensor("gam", [128, NFC], F32, kind="ExternalInput")
    bet_d = nc.dram_tensor("bet", [128, NFC], F32, kind="ExternalInput")
    wgu_d = nc.dram_tensor("wgu", [NE, NJJ, 2, NFC, 128, 128], F32, kind="ExternalInput")
    wd_d = nc.dram_tensor("wd", [NE, 2, HIDSLOTS, 128, D], F32, kind="ExternalInput")
    sel_d = nc.dram_tensor("sel", [NE, NE * 128], F32, kind="ExternalInput")
    idn_d = nc.dram_tensor("idn", [128, 128], F32, kind="ExternalInput")
    y_d = nc.dram_tensor("y", [NTC, 128, D], F32, kind="ExternalOutput")
    if use_cc:
        ccin_d = nc.dram_tensor("ccin", [NLC, 128, D], F32)
        ccout_d = nc.dram_tensor("ccout", [NLC, 128, D], F32)

    resid = sb("resid", [128, NTC * D], F32, ctxs)
    gub = [sb(f"gub{k}", [128, 2 * D], BF16, ctxs) for k in range(3)]
    wdb = [sb("wdb0", [128, HIDSLOTS * D], BF16, ctxs)]
    rw = sb("s_rw", [128, NE * NFC], BF16, ctxs)
    gam = sb("s_gam", [128, NFC], F32, ctxs)
    bet = sb("s_bet", [128, NFC], F32, ctxs)
    idn = sb("s_idn", [128, 128], F32, ctxs)
    sel = sb("s_sel", [NE, NE * 128], F32, ctxs)
    sums = sb("sums", [128, NTC], F32, ctxs)
    ssq = sb("ssq", [128, NTC], F32, ctxs)
    mu = sb("mu", [128, NTC], F32, ctxs)
    tmp16 = sb("tmp16", [128, NTC], F32, ctxs)
    rstd = sb("rstd", [128, NTC], F32, ctxs)
    ratio = sb("ratio", [128, NTC], F32, ctxs)
    mu1 = sb("mu1", [128, NTC], F32, ctxs)
    r2x = sb("r2x", [128, NTC], F32, ctxs)

    for s in ("sT", "sV", "sS", "wcon", "hcon", "hx", "out0", "out1",
              "xtg0", "xtg1", "gub0", "gub1", "gub2", "wdb0", "wth1",
              "ccup", "ccsm", "ccdn"):
        b.sem(s, ctxs)

    PSh = nc.psum_tensor("PS", [128, 4096], F32)
    ctxs.append(PSh)
    PS = PSh.__enter__()

    def bank(i, off=0, n=512):
        return PS[:, i * 512 + off:i * 512 + off + n]

    def gdma(sem, dst_ap, src_ap):
        b.op("gpsimd", lambda e, d=dst_ap, s=src_ap, h=b.sems[sem]:
             e.dma_start(d, s).then_inc(h, 16))
        return (sem, b.inc(sem, 16))

    def sdma(sem, dst_ap, src_ap, slow=False):
        b.op("sync", lambda e, d=dst_ap, s=src_ap, h=b.sems[sem], sl=slow:
             e.dma_start(d, s, allow_slow_non_contiguous=sl).then_inc(h, 16))
        return (sem, b.inc(sem, 16))

    def mm(out_ap, lh_ap, rh_ap, start, stop, inc=False):
        if inc:
            b.op("tensor", lambda t, o=out_ap, l=lh_ap, r=rh_ap, st=start, sp=stop,
                 h=b.sems["sT"]: t.matmul(o, l, r, start=st, stop=sp).then_inc(h, 1))
            return b.inc("sT")
        b.op("tensor", lambda t, o=out_ap, l=lh_ap, r=rh_ap, st=start, sp=stop:
             t.matmul(o, l, r, start=st, stop=sp))
        return None

    def vop(fn):
        b.op("vector", fn)

    def vop_inc(fn):
        b.op("vector", lambda v, f=fn, h=b.sems["sV"]: f(v).then_inc(h, 1))
        return b.inc("sV")

    def sop(fn):
        b.op("scalar", fn)

    def sop_inc(fn):
        b.op("scalar", lambda s, f=fn, h=b.sems["sS"]: f(s).then_inc(h, 1))
        return b.inc("sS")

    def reshape3(t_ap, c):
        return t_ap.rearrange("p (c n) -> p c n", c=c)


    def keepalive(points):
        if not points:
            return
        b.wpair("tensor", getattr(b, "v_hcon", None))
        n = len(points)
        for k, (sem, val) in enumerate(points):
            b.wait("tensor", sem, val)
            mm(bank(7, 0, 128), idn[0:1, 0:128], idn[0:1, 0:128],
               start=(k == 0), stop=(k == n - 1), inc=True)
        b.bank_free[7] = ("sT", b.cnt["sT"])

    actxs = []
    xtg = [sb(f"xtg{k}", [128, NFC * GRP], BF16, actxs) for k in range(2)]
    phg = sb("phg", [128, NGT * L], BF16, actxs)
    gg = sb("gg", [128, NGT * D], BF16, actxs)
    theta = sb("theta", [128, NLC * T], BF16, actxs)
    macc = sb("macc", [128, NLC * D], F32, actxs)
    msb = sb("msb", [128, NLC * D], BF16, actxs)
    wth = sb("s_wth", [128, NFC * L], BF16, actxs)
    wph = sb("s_wph", [128, NFC * L], BF16, actxs)
    wg = sb("s_wg", [128, NFC * D], BF16, actxs)

    v_wth = gdma("wth1", reshape3(wth[:, :], NFC),
                 wth_d[:, :, :].rearrange("c p n -> p c n"))
    sdma("hcon", gam[:, :], gam_d[:, :])
    sdma("hcon", bet[:, :], bet_d[:, :])
    sdma("hcon", idn[:, :], idn_d[:, :])
    v_hcon = sdma("hcon", sel[:, :], sel_d[:, :])
    b.v_hcon = v_hcon
    v_x = sdma("hx", reshape3(resid[:, :], NTC),
               xtok[:, :, :].rearrange("c p n -> p c n"))

    xtg_ready = [None, None]

    def load_group(g):
        k = g % 2
        b.wpair("gpsimd", b.buf_free.get(("xtg", k)))
        xtg_ready[k] = gdma(
            f"xtg{k}", reshape3(xtg[k][:, :], NFC),
            xT_d[:, :, g * GRP:(g + 1) * GRP].rearrange("c p n -> p c n"))

    load_group(0)
    gdma("wcon", reshape3(wph[:, :], NFC), wph_d[:, :, :].rearrange("c p n -> p c n"))
    gdma("wcon", reshape3(wg[:, :], NFC), wg_d[:, :, :].rearrange("c p n -> p c n"))
    v_wcon = gdma("wcon", reshape3(rw[:, :], NFC),
                  rw_d[:, :, :].rearrange("c p n -> p c n"))
    if NGR > 1:
        load_group(1)

    b.wpair("tensor", v_wth)
    mev_prev = None
    ev_prev_group = None
    for g in range(NGR):
        k = g % 2
        b.wpair("tensor", xtg_ready[k])
        own = g < NOG
        if own:
            for gs in range(NGS):
                b.wpair("tensor", b.bank_free.get(6))
                b.wpair("tensor", b.bank_free.get(7))
                for fc in range(NFC):
                    for lc in range(NLC):
                        mm(bank(6 + lc, 0, GSL),
                           wth[:, fc * L + lc * 128: fc * L + lc * 128 + 128],
                           xtg[k][:, fc * GRP + gs * GSL: fc * GRP + (gs + 1) * GSL],
                           start=(fc == 0), stop=(fc == NFC - 1),
                           inc=(fc == NFC - 1 and lc == NLC - 1))
                tmil = b.cnt["sT"]
                b.wait("vector", "sT", tmil)
                th_ev = b.vbatch([
                    (lambda v, c=lc * T + g * GRP + gs * GSL, lc=lc, n=GSL:
                     v.tensor_copy(theta[:, c:c + n], bank(6 + lc, 0, n)))
                    for lc in range(NLC)])
                b.bank_free[6] = ("sV", th_ev)
                b.bank_free[7] = ("sV", th_ev)
        b.wpair("tensor", v_wcon)
        if ev_prev_group is not None:
            b.wait("vector", "sV", ev_prev_group)
        ev = [None] * NGT
        ev_last = None
        for tci in range(NGT):
            bs = (tci % 2) * 3
            for bb in (bs, bs + 1, bs + 2):
                b.wpair("tensor", b.bank_free.get(bb))
            for fc in range(NFC):
                lh = xtg[k][:, fc * GRP + tci * 128: fc * GRP + tci * 128 + 128]
                st, sp = (fc == 0), (fc == NFC - 1)
                mm(bank(bs, 0, L), lh, wph[:, fc * L:(fc + 1) * L], st, sp)
                mm(bank(bs + 1), lh, wg[:, fc * D:fc * D + 512], st, sp)
                mm(bank(bs + 2), lh, wg[:, fc * D + 512:(fc + 1) * D], st, sp,
                   inc=sp)
            tmil = b.cnt["sT"]
            b.wait("vector", "sT", tmil)
            vop(lambda v, tci=tci, bs=bs:
                v.tensor_copy(phg[:, tci * L:(tci + 1) * L], bank(bs, 0, L)))
            vop(lambda v, tci=tci, bs=bs:
                v.tensor_copy(gg[:, tci * D:tci * D + 512], bank(bs + 1)))
            ev_last = vop_inc(lambda v, tci=tci, bs=bs:
                              v.tensor_copy(gg[:, tci * D + 512:(tci + 1) * D],
                                            bank(bs + 2)))
            ev[tci] = ev_last
            for bb in (bs, bs + 1, bs + 2):
                b.bank_free[bb] = ("sV", ev_last)
        ev_prev_group = ev_last
        b.buf_free[("xtg", k)] = ("sT", b.cnt["sT"])
        if g + 2 < NGR:
            load_group(g + 2)
        for lc in range(NLC):
            b.wpair("tensor", b.bank_free.get(6))
            b.wpair("tensor", b.bank_free.get(7))
            for tci in range(NGT):
                b.wait("tensor", "sV", ev[tci])
                for oh in range(2):
                    mm(bank(6 + oh),
                       phg[:, tci * L + lc * 128: tci * L + lc * 128 + 128],
                       gg[:, tci * D + oh * 512: tci * D + oh * 512 + 512],
                       start=(tci == 0), stop=(tci == NGT - 1),
                       inc=(tci == NGT - 1 and oh == 1))
            tmil = b.cnt["sT"]
            b.wait("vector", "sT", tmil)
            b.wait("vector", "sV", mev_prev)
            if g == 0:
                mev_prev = b.vbatch([
                    (lambda v, c=lc * D + oh * 512, bb=6 + oh:
                     v.tensor_copy(macc[:, c:c + 512], bank(bb)))
                    for oh in range(2)])
            else:
                mev_prev = b.vbatch([
                    (lambda v, c=lc * D + oh * 512, bb=6 + oh:
                     v.tensor_add(macc[:, c:c + 512], bank(bb), macc[:, c:c + 512]))
                    for oh in range(2)])
            b.bank_free[6] = ("sV", mev_prev)
            b.bank_free[7] = ("sV", mev_prev)

    if use_cc:
        b.wait("sync", "sV", mev_prev)
        v_ccup = sdma("ccup", ccin_d[:, :, :].rearrange("c p n -> p c n"),
                      reshape3(macc[:, :], NLC))
        b.wpair("gpsimd", v_ccup)
        b.op("gpsimd", lambda g, h=b.sems["ccsm"]: g.collective_compute(
            "AllReduce", mybir.AluOpType.add,
            replica_groups=[[0, 1, 2, 3], [4, 5, 6, 7]],
            ins=[ccin_d.ap().opt()], outs=[ccout_d.ap().opt()]).then_inc(h, 1))
        b.inc("ccsm")
        b.wait("gpsimd", "ccsm", 1)
        b.op("gpsimd", lambda g, h=b.sems["ccdn"]: g.dma_start(
            out=reshape3(macc[:, :], NLC),
            in_=ccout_d[:, :, :].rearrange("c p n -> p c n")).then_inc(h, 16))
        v_ccdn = ("ccdn", b.inc("ccdn", 16))
        b.wpair("scalar", v_ccdn)
        b.wpair("tensor", getattr(b, "v_hcon", None))
        b.wait("tensor", "sV", mev_prev)
        NSPIN = 60
        for k in range(NSPIN):
            mm(bank(7, 0, 128), idn[0:1, 0:128], idn[0:1, 0:128],
               start=(k == 0), stop=(k == NSPIN - 1), inc=(k == NSPIN - 1))
        b.bank_free[7] = ("sT", b.cnt["sT"])

    b.wait("scalar", "sV", mev_prev)
    v_msb = sop_inc(lambda s: s.activation(msb[:, :], macc[:, :], AF.Copy,
                                           bias=0.0, scale=float(SCALE)))

    b.wait("tensor", "sS", v_msb)
    b.wpair("vector", v_x)
    h1_done = None
    for tc in range(NTC):
        bs = (tc % 2) * 2
        b.wpair("tensor", b.bank_free.get(bs))
        b.wpair("tensor", b.bank_free.get(bs + 1))
        for oh in range(2):
            for lc in range(NLC):
                mm(bank(bs + oh),
                   theta[:, lc * T + tc * 128: lc * T + tc * 128 + 128],
                   msb[:, lc * D + oh * 512: lc * D + oh * 512 + 512],
                   start=(lc == 0), stop=(lc == NLC - 1),
                   inc=(lc == NLC - 1 and oh == 1))
        tmil = b.cnt["sT"]
        b.wait("vector", "sT", tmil)
        h1_done = b.vbatch([
            (lambda v, c=tc * D + oh * 512, bb=bs + oh:
             v.tensor_add(resid[:, c:c + 512], bank(bb), resid[:, c:c + 512]))
            for oh in range(2)])
        b.bank_free[bs] = ("sV", h1_done)
        b.bank_free[bs + 1] = ("sV", h1_done)

    def ln_stats(chunks, src, dump_sq, dump_sum, v_pre, t_pre, keep=False,
                 with_ratio=False):
        n = len(chunks)
        b.wait("scalar", "sV", v_pre)
        b.wait("scalar", "sT", t_pre)
        c0s = b.cnt["sS"]
        s_sq = b.chain("scalar", [
            (lambda s, a=src(tc), i=i, dq=dump_sq:
             s.activation(dq, a, AF.Square, accum_out=ssq[:, i:i + 1]))
            for i, tc in enumerate(chunks)])
        b.wait("vector", "sV", v_pre)
        b.wait("vector", "sT", t_pre)
        fns = [
            (lambda v, a=src(tc), i=i, ds=dump_sum:
             v.tensor_scalar(ds, a, 0.0, None, ALU.add, ALU.add,
                             accum_out=sums[:, i:i + 1]))
            for i, tc in enumerate(chunks)]
        fns += [
            lambda v: v.tensor_scalar_mul(mu[:, 0:n], sums[:, 0:n], 1.0 / D),
            lambda v: v.tensor_mul(tmp16[:, 0:n], mu[:, 0:n], mu[:, 0:n]),
        ]
        c0 = b.cnt["sV"]
        p1 = b.chain("vector", fns)
        if keep:
            pts = [("sS", v) for v in range(c0s + 1, s_sq + 1)]
            pts += [("sV", v) for v in range(c0 + 2, p1 + 1, 2)]
            keepalive(pts)
        b.wait("vector", "sV", p1)
        b.wait("vector", "sS", s_sq)
        v_var = b.chain("vector", [
            lambda v: v.scalar_tensor_tensor(tmp16[:, 0:n], ssq[:, 0:n], 1.0 / D,
                                             tmp16[:, 0:n], ALU.mult, ALU.subtract),
            lambda v: v.tensor_scalar_add(tmp16[:, 0:n], tmp16[:, 0:n], float(LN_EPS)),
            lambda v: v.reciprocal(rstd[:, 0:n], tmp16[:, 0:n]),
        ])
        if with_ratio:
            v_var = b.chain("vector", [
                lambda v: v.tensor_scalar(sums[:, 0:n], tmp16[:, 0:n],
                                          1.0 + float(LN_EPS), -float(LN_EPS),
                                          ALU.mult, ALU.add),
                lambda v: v.reciprocal(ssq[:, 0:n], sums[:, 0:n]),
                lambda v: v.tensor_mul(ratio[:, 0:n], tmp16[:, 0:n], ssq[:, 0:n]),
            ])
        b.wait("scalar", "sV", v_var)
        if with_ratio:
            sop(lambda s, n=n: s.activation(ratio[:, 0:n], ratio[:, 0:n], AF.Sqrt))
        s_r = sop_inc(lambda s, n=n: s.activation(rstd[:, 0:n], rstd[:, 0:n], AF.Sqrt))
        b.wait("vector", "sS", s_r)
        if with_ratio:
            b.chain("vector", [
                lambda v, n=n: v.tensor_mul(ratio[:, 0:n], ratio[:, 0:n],
                                            rstd[:, 0:n])])
        return s_r

    def layernorm(chunks, src, dst, dump_sq, dump_sum, v_pre, t_pre, keep=False,
                  with_ratio=False):
        s_r = ln_stats(chunks, src, dump_sq, dump_sum, v_pre, t_pre, keep=keep,
                       with_ratio=with_ratio)
        b.s_r_last = s_r
        return b.vbatch([
            (lambda v, a=src(tc), d=dst(tc), i=i:
             v.tensor_scalar(d, a, mu[:, i:i + 1], rstd[:, i:i + 1],
                             ALU.subtract, ALU.mult))
            for i, tc in enumerate(chunks)])

    s_r1 = ln_stats(
        list(range(NTC)),
        src=lambda tc: resid[:, tc * D:(tc + 1) * D],
        dump_sq=gg[:, 0:D], dump_sum=gg[:, D:2 * D],
        v_pre=h1_done, t_pre=None, keep=True, with_ratio=True)
    b.s_r_last = s_r1
    v_zsc = b.chain("vector", [
        lambda v: v.tensor_copy(mu1[:, 0:NTC], mu[:, 0:NTC]),
        lambda v: v.tensor_scalar_mul(r2x[:, 0:NTC], rstd[:, 0:NTC], 2.0),
    ])
    b.v_zsc = v_zsc

    NU = 2 * NE * NJJ
    NW = 2 * NE * 2

    def unit_src(n):
        e = (n // NJJ) % NE
        jj = n % NJJ
        return wgu_d[e, jj, :, :, :, :].rearrange("a c p n -> p a c n")

    def wd_src(m):
        e = (m // 2) % NE
        p = m % 2
        return wd_d[e, p, :, :, :].rearrange("c p n -> p c n")

    gu_land = {}
    wd_land = {}

    def load_gu(n):
        k = n % 3
        b.wpair("gpsimd", b.buf_free.get(("gub", k)))
        gu_land[n] = gdma(
            f"gub{k}", gub[k][:, :].rearrange("p (a c n) -> p a c n", a=2, c=NFC),
            unit_src(n))

    def load_wd(m):
        b.wpair("gpsimd", b.buf_free.get(("wdb",)))
        wd_land[m] = gdma("wdb0", reshape3(wdb[0][:, :], HIDSLOTS), wd_src(m))

    load_gu(0)
    load_gu(1)
    load_gu(2)
    load_wd(0)

    blkA = nc.Block()
    blkA_g = blkA.__enter__()
    b.emit_into(blkA_g)
    blkA.__exit__(None, None, None)

    for c in reversed(actxs):
        c.__exit__(None, None, None)

    bctxs = []
    xnT = sb("xnT", [128, NFC * TH], BF16, bctxs)
    xe = sb("xe", [128, NFC * TH], BF16, bctxs)
    hid = sb("hid", [128, 2 * HIDSLOTS * TH], BF16, bctxs)
    acc = sb("acc", [128, NTH * D], F32, bctxs)
    z2 = sb("z2", [128, D], F32, bctxs)
    srep = sb("srep", [128, TH], BF16, bctxs)
    sg = [sb("sg0", [128, TH], BF16, bctxs)]
    logits = sb("logits", [128, NTH * NE], F32, bctxs)
    probs = sb("probs", [128, NTH * NE], F32, bctxs)
    mk = sb("mk", [128, NTH * NE], F32, bctxs)
    gates = mk
    gatesT = sb("gatesT", [NE, TH], F32, bctxs)
    rst8 = sums
    rst8b = ssq

    unit_ctr = 0
    wd_ctr = 0
    out_dma = [None, None]
    hid_ready = {}
    last_down_mil = None

    for h in range(2):
        hc0 = h * NTH
        b.wpair("scalar", v_hcon)
        b.wpair("vector", v_hcon)
        b.wait("vector", "sV", b.v_zsc)
        b.wait("vector", "sS", b.s_r_last)

        xnT_done = None
        for i in range(NTH):
            tc = hc0 + i
            zs = 0
            b.wpair("vector", b.buf_free.get(("z", zs)))
            v_z = vop_inc(lambda v, tc=tc, zs=zs:
                          v.tensor_scalar(z2[:, zs * D:(zs + 1) * D],
                                          resid[:, tc * D:(tc + 1) * D],
                                          mu1[:, tc:tc + 1], ratio[:, tc:tc + 1],
                                          ALU.subtract, ALU.mult))
            keepalive([("sV", v_z)])
            b.wpair("tensor", v_hcon)
            bs = (i % 2) * 2
            b.wpair("tensor", b.bank_free.get(bs))
            b.wpair("tensor", b.bank_free.get(bs + 1))
            for fc in range(NFC):
                bb = bs + fc // 4
                off = (fc % 4) * 128
                if fc < NFC - 1:
                    b.op("tensor", lambda t, zs=zs, fc=fc, bb=bb, off=off:
                         t.transpose(bank(bb, off, 128),
                                     z2[:, zs * D + fc * 128: zs * D + (fc + 1) * 128],
                                     idn[:, :]))
                else:
                    tmil = b.op_inc("tensor", lambda t, zs=zs, fc=fc, bb=bb, off=off:
                                    t.transpose(bank(bb, off, 128),
                                                z2[:, zs * D + fc * 128: zs * D + (fc + 1) * 128],
                                                idn[:, :]))
            b.buf_free[("z", zs)] = ("sT", tmil)
            b.wait("scalar", "sT", tmil)
            b.wpair("scalar", v_hcon)
            for fc in range(NFC):
                bb = bs + fc // 4
                off = (fc % 4) * 128
                if fc < NFC - 1:
                    sop(lambda s, fc=fc, bb=bb, off=off, i=i: s.activation(
                        xnT[:, fc * TH + i * 128: fc * TH + (i + 1) * 128],
                        bank(bb, off, 128), AF.Identity,
                        bias=bet[:, fc:fc + 1], scale=gam[:, fc:fc + 1]))
                else:
                    xnT_done = b.op_inc("scalar", lambda s, fc=fc, bb=bb, off=off, i=i:
                                        s.activation(
                        xnT[:, fc * TH + i * 128: fc * TH + (i + 1) * 128],
                        bank(bb, off, 128), AF.Identity,
                        bias=bet[:, fc:fc + 1], scale=gam[:, fc:fc + 1]))
            b.bank_free[bs] = ("sS", xnT_done)
            b.bank_free[bs + 1] = ("sS", xnT_done)
            b.wait("tensor", "sS", xnT_done)
            b.wpair("tensor", v_wcon)
            b.wpair("tensor", b.bank_free.get(4))
            for fc in range(NFC):
                mm(bank(4, 0, NE), xnT[:, fc * TH + i * 128: fc * TH + (i + 1) * 128],
                   rw[:, fc * NE:(fc + 1) * NE],
                   start=(fc == 0), stop=(fc == NFC - 1), inc=(fc == NFC - 1))
            tmil = b.cnt["sT"]
            b.wait("vector", "sT", tmil)
            v_lg = vop_inc(lambda v, i=i: v.tensor_copy(
                logits[:, i * NE:(i + 1) * NE], bank(4, 0, NE)))
            b.bank_free[4] = ("sV", v_lg)


        lg3 = reshape3(logits[:, 0:NTH * NE], NTH)
        pr3 = reshape3(probs[:, 0:NTH * NE], NTH)
        mk3 = reshape3(mk[:, 0:NTH * NE], NTH)

        def bc(stat):
            s3 = stat[:, 0:NTH].rearrange("p (c o) -> p c o", o=1)
            _, out = bass.broadcast_tensor_aps(lg3, s3)
            return out

        b.wait("vector", "sV", v_lg)
        c0r = b.cnt["sV"]
        v_sub = b.chain("vector", [
            lambda v: v.tensor_scalar(logits[:, 0:NTH * NE], logits[:, 0:NTH * NE],
                                      float(CLAMP), float(-CLAMP), ALU.min, ALU.max),
            lambda v: v.tensor_reduce(
                rst8[:, 0:NTH].rearrange("p (c o) -> p c o", o=1), lg3, AX, ALU.max),
            lambda v: v.tensor_sub(lg3, lg3, bc(rst8)),
        ])
        b.wait("scalar", "sV", v_sub)
        s_exp = sop_inc(lambda s: s.activation(logits[:, 0:NTH * NE],
                                               logits[:, 0:NTH * NE], AF.Exp))
        b.wait("vector", "sS", s_exp)
        v_gt = b.chain("vector", [
            lambda v: v.tensor_reduce(
                rst8[:, 0:NTH].rearrange("p (c o) -> p c o", o=1), lg3, AX, ALU.add),
            lambda v: v.tensor_scalar_add(rst8[:, 0:NTH], rst8[:, 0:NTH],
                                          float(ROUTER_EPS)),
            lambda v: v.reciprocal(rst8b[:, 0:NTH], rst8[:, 0:NTH]),
            lambda v: v.tensor_mul(pr3, lg3, bc(rst8b)),
            lambda v: v.tensor_scalar(probs[:, 0:NTH * NE], probs[:, 0:NTH * NE],
                                      float(ROUTER_EPS), 1.0, ALU.max, ALU.min),
            lambda v: v.tensor_reduce(
                rst8[:, 0:NTH].rearrange("p (c o) -> p c o", o=1), pr3, AX, ALU.max),
            lambda v: v.tensor_tensor(mk3, pr3, bc(rst8), ALU.is_ge),
            lambda v: v.scalar_tensor_tensor(mk[:, 0:NTH * NE], mk[:, 0:NTH * NE],
                                             -1e30, probs[:, 0:NTH * NE],
                                             ALU.mult, ALU.add),
            lambda v: v.tensor_reduce(
                rst8b[:, 0:NTH].rearrange("p (c o) -> p c o", o=1), mk3, AX, ALU.max),
            lambda v: v.tensor_tensor(mk3, pr3, bc(rst8b), ALU.is_ge),
            lambda v: v.tensor_mul(gates[:, 0:NTH * NE],
                                   probs[:, 0:NTH * NE], mk[:, 0:NTH * NE]),
        ])
        keepalive([("sV", v) for v in range(c0r + 2, v_gt + 1, 3)])

        b.wait("tensor", "sV", v_gt)
        b.wpair("tensor", b.bank_free.get(4))
        b.wpair("tensor", b.bank_free.get(5))
        tmil = None
        for i in range(NTH):
            bb = 4 + i // 4
            off = (i % 4) * 128
            if i < NTH - 1:
                b.op("tensor", lambda t, i=i, bb=bb, off=off:
                     t.transpose(PS[0:NE, bb * 512 + off:bb * 512 + off + 128],
                                 gates[:, i * NE:(i + 1) * NE], idn[:, :]))
            else:
                tmil = b.op_inc("tensor", lambda t, i=i, bb=bb, off=off:
                                t.transpose(PS[0:NE, bb * 512 + off:bb * 512 + off + 128],
                                            gates[:, i * NE:(i + 1) * NE], idn[:, :]))
        b.wait("vector", "sT", tmil)
        b.wait("vector", "sV", v_gt)
        gT_done = b.vbatch([
            (lambda v, bb=bb, w=min(512, NTH * 128 - (bb - 4) * 512):
             v.tensor_copy(gatesT[:, (bb - 4) * 512:(bb - 4) * 512 + w],
                           PS[0:NE, bb * 512:bb * 512 + w]))
            for bb in range(4, 4 + (NTH * 128 + 511) // 512)])
        b.bank_free[4] = ("sV", gT_done)
        b.bank_free[5] = ("sV", gT_done)

        def emit_srep_xe(e, banks):
            b.wait("tensor", "sV", gT_done)
            b.wpair("tensor", v_hcon)
            for slc in range(NSL):
                b.wpair("tensor", b.bank_free.get(banks[slc]))
                mm(bank(banks[slc], 0, SL), sel[:, e * 128:(e + 1) * 128],
                   gatesT[:, slc * SL:(slc + 1) * SL], start=True, stop=True,
                   inc=(slc == NSL - 1))
            tmil = b.cnt["sT"]
            b.wait("vector", "sT", tmil)
            b.wpair("vector", b.buf_free.get(("srep",)))
            v_srep = b.vbatch([
                (lambda v, slc=slc, bk=banks[slc]: v.tensor_copy(
                    srep[:, slc * SL:(slc + 1) * SL], bank(bk, 0, SL)))
                for slc in range(NSL)])
            for slc in range(NSL):
                b.bank_free[banks[slc]] = ("sV", v_srep)
            b.wpair("vector", b.buf_free.get(("xe",)))
            b.wait("vector", "sS", xnT_done)
            b.wait("vector", "sV", v_srep)
            c0x = b.cnt["sV"]
            v_xe = b.chain("vector", [
                (lambda v, fc=fc: v.tensor_mul(
                    xe[:, fc * TH:(fc + 1) * TH], xnT[:, fc * TH:(fc + 1) * TH],
                    srep[:, 0:TH]))
                for fc in range(NFC)])
            if e == 0:
                keepalive([("sV", v) for v in range(c0x + 2, v_xe + 1, 2)])
            b.buf_free[("srep",)] = ("sV", v_xe)
            return v_xe

        v_xe = emit_srep_xe(0, banks=(2, 3))

        def gu_pass(p, e):
            nonlocal unit_ctr
            hset = (p % 2) * HIDSLOTS
            for j2 in range(HIDSLOTS):
                uk = unit_ctr % 3
                b.wpair("tensor", gu_land[unit_ctr])
                b.wpair("tensor", b.buf_free.get(("hid", p % 2, j2)))
                gb = 4 if j2 % 2 == 0 else 0
                ub = 6 if j2 % 2 == 0 else 2
                mils = {}
                for a, bs in ((0, gb), (1, ub)):
                    for slc in range(NSL):
                        b.wpair("tensor", b.bank_free.get(bs + slc))
                    for fc in range(NFC):
                        for slc in range(NSL):
                            mm(bank(bs + slc, 0, SL),
                               gub[uk][:, a * D + fc * 128:a * D + (fc + 1) * 128],
                               xe[:, fc * TH + slc * SL: fc * TH + (slc + 1) * SL],
                               start=(fc == 0), stop=(fc == NFC - 1),
                               inc=(fc == NFC - 1 and slc == NSL - 1))
                    mils[a] = b.cnt["sT"]
                b.buf_free[("gub", uk)] = ("sT", mils[1])
                if unit_ctr + 3 < NU:
                    load_gu(unit_ctr + 3)
                sgk = sg[0]
                b.wait("scalar", "sT", mils[0])
                b.wpair("scalar", b.buf_free.get(("sg",)))
                for slc in range(NSL):
                    if slc < NSL - 1:
                        sop(lambda s, slc=slc, sgk=sgk, gb=gb: s.activation(
                            sgk[:, slc * SL:(slc + 1) * SL], bank(gb + slc, 0, SL),
                            AF.Silu))
                    else:
                        s_silu = b.op_inc("scalar", lambda s, slc=slc, sgk=sgk, gb=gb:
                                          s.activation(sgk[:, slc * SL:(slc + 1) * SL],
                                                       bank(gb + slc, 0, SL), AF.Silu))
                for slc in range(NSL):
                    b.bank_free[gb + slc] = ("sS", s_silu)
                b.wait("vector", "sT", mils[1])
                b.wait("vector", "sS", s_silu)
                v_hid = vop_inc(lambda v, c=(hset + j2) * TH, sgk=sgk, ub=ub:
                                v.tensor_mul(hid[:, c:c + TH], sgk[:, 0:TH],
                                             PS[:, ub * 512: ub * 512 + TH]))
                hid_ready[(p % 2, j2)] = v_hid
                b.buf_free[("sg",)] = ("sV", v_hid)
                for slc in range(NSL):
                    b.bank_free[ub + slc] = ("sV", v_hid)
                unit_ctr += 1
            return b.cnt["sT"]

        def down_pass(p, first, srep_fn=None):
            nonlocal wd_ctr, acc_ev, last_down_mil
            hset = (p % 2) * HIDSLOTS
            b.wpair("tensor", wd_land[wd_ctr])
            b.wpair("vector", b.buf_free.get(("accw",)))
            pairs = (4, 6, 0)
            srep_at = min(2, NTH - 1)
            for i in range(NTH):
                if i == srep_at and srep_fn is not None:
                    srep_fn()
                bs = pairs[i % 3]
                b.wpair("tensor", b.bank_free.get(bs))
                b.wpair("tensor", b.bank_free.get(bs + 1))
                for ffc in range(HIDSLOTS):
                    b.wait("tensor", "sV", hid_ready[(p % 2, ffc)])
                    for oh in range(2):
                        mm(bank(bs + oh),
                           hid[:, (hset + ffc) * TH + i * 128:
                               (hset + ffc) * TH + (i + 1) * 128],
                           wdb[0][:, ffc * D + oh * 512: ffc * D + oh * 512 + 512],
                           start=(ffc == 0), stop=(ffc == HIDSLOTS - 1),
                           inc=(ffc == HIDSLOTS - 1 and oh == 1))
                tmil = b.cnt["sT"]
                b.wait("vector", "sT", tmil)
                if h == 1 and first and i == 0:
                    b.wpair("vector", out_dma[0])
                if first:
                    acc_ev = b.vbatch([
                        (lambda v, c=i * D + oh * 512, bb=bs + oh:
                         v.tensor_copy(acc[:, c:c + 512], bank(bb)))
                        for oh in range(2)])
                else:
                    acc_ev = b.vbatch([
                        (lambda v, c=i * D + oh * 512, bb=bs + oh:
                         v.tensor_add(acc[:, c:c + 512], bank(bb),
                                      acc[:, c:c + 512]))
                        for oh in range(2)])
                b.bank_free[bs] = ("sV", acc_ev)
                b.bank_free[bs + 1] = ("sV", acc_ev)
            b.buf_free[("accw",)] = ("sV", acc_ev)
            last_down_mil = b.cnt["sT"]
            for j2 in range(HIDSLOTS):
                b.buf_free[("hid", p % 2, j2)] = ("sT", last_down_mil)
            b.buf_free[("wdb",)] = ("sT", last_down_mil)
            if wd_ctr + 1 < NW:
                load_wd(wd_ctr + 1)
            wd_ctr += 1

        acc_ev = None
        pending = None
        for p in range(2 * NE):
            e, hp = p // 2, p % 2
            if hp == 0:
                b.wait("tensor", "sV", v_xe)
            gu_last = gu_pass(p, e)
            srep_fn = None
            if hp == 1 and e + 1 < NE:
                b.buf_free[("xe",)] = ("sT", gu_last)

                def srep_fn(e=e):
                    nonlocal v_xe
                    v_xe = emit_srep_xe(e + 1, banks=(2, 3))
            if pending is not None:
                down_pass(pending, first=(pending == 0), srep_fn=srep_fn)
            pending = p
        down_pass(pending, first=False, srep_fn=None)

        b.wait("vector", "sV", acc_ev)
        fns_h2 = []
        for i in range(NTH):
            tc = hc0 + i
            fns_h2.append(lambda v, tc=tc: v.tensor_scalar(
                resid[:, tc * D:(tc + 1) * D], resid[:, tc * D:(tc + 1) * D],
                mu1[:, tc:tc + 1], r2x[:, tc:tc + 1], ALU.subtract, ALU.mult))
            fns_h2.append(lambda v, tc=tc, i=i: v.tensor_add(
                resid[:, tc * D:(tc + 1) * D], resid[:, tc * D:(tc + 1) * D],
                acc[:, i * D:(i + 1) * D]))
        h2_done = b.chain("vector", fns_h2)
        ln3_done = layernorm(
            list(range(hc0, hc0 + NTH)),
            src=lambda tc: resid[:, tc * D:(tc + 1) * D],
            dst=lambda tc: acc[:, (tc - hc0) * D:(tc - hc0 + 1) * D],
            dump_sq=hid[:, 0:D], dump_sum=z2[:, 0:D],
            v_pre=h2_done, t_pre=last_down_mil, keep=(h == 0))
        b.wait("sync", "sV", ln3_done)
        out_dma[h] = sdma(f"out{h}", y_d[hc0:hc0 + NTH, :, :].rearrange("c p n -> p c n"),
                          reshape3(acc[:, 0:NTH * D], NTH))

    b.wpair("sync", out_dma[0])
    b.wpair("sync", out_dma[1])

    blkB = nc.Block()
    blkB_g = blkB.__enter__()
    b.emit_into(blkB_g)
    blkB.__exit__(None, None, None)

    for c in reversed(bctxs):
        c.__exit__(None, None, None)
    for c in reversed(ctxs):
        c.__exit__(None, None, None)
    return nc


def kernel(L0_attn_x, L0_attn_W_theta, L0_attn_W_phi, L0_attn_W_g,
           L0_ffn_ln_gamma, L0_ffn_ln_beta, L0_ffn_router_w,
           L0_ffn_gate_up_w, L0_ffn_down_w):
    x = np.ascontiguousarray(np.asarray(L0_attn_x, dtype=np.float32))
    B, S, _ = x.shape
    NCORES = 8
    CPB = NCORES // B
    T = S // CPB

    def wT_chunks(w):
        return np.ascontiguousarray(np.asarray(w, np.float32).T.reshape(NFC, 128, -1))

    wth_h = wT_chunks(L0_attn_W_theta)
    wph_h = wT_chunks(L0_attn_W_phi)
    wg_h = wT_chunks(L0_attn_W_g)
    rw_h = wT_chunks(L0_ffn_router_w)
    gam_h = np.ascontiguousarray(np.asarray(L0_ffn_ln_gamma, np.float32).reshape(NFC, 128).T)
    bet_h = np.ascontiguousarray(np.asarray(L0_ffn_ln_beta, np.float32).reshape(NFC, 128).T)
    wgu = np.asarray(L0_ffn_gate_up_w, np.float32)
    wgu_h = np.ascontiguousarray(
        wgu.reshape(NE, 2, NJJ, 128, D).transpose(0, 2, 1, 4, 3)
           .reshape(NE, NJJ, 2, NFC, 128, 128))
    wd = np.asarray(L0_ffn_down_w, np.float32)
    wd_h = np.ascontiguousarray(
        wd.transpose(0, 2, 1).reshape(NE, 2, HIDSLOTS, 128, D))
    sel_h = np.zeros((NE, NE * 128), np.float32)
    for e in range(NE):
        sel_h[e, e * 128:(e + 1) * 128] = 1.0
    idn_h = np.eye(128, dtype=np.float32)

    xT_batches = [np.ascontiguousarray(x[bb].T) for bb in range(B)]

    nc = build_program(T=T, TB=S, GRP=512)
    in_maps = []
    for c in range(NCORES):
        bb = c // CPB
        s0 = (c % CPB) * T
        xt = np.ascontiguousarray(x[bb, s0:s0 + T].reshape(T // 128, 128, D))
        xTb = xT_batches[bb]
        if USE_CC:
            xT_core = np.ascontiguousarray(xTb[:, s0:s0 + T].reshape(NFC, 128, T))
        else:
            xT_core = np.ascontiguousarray(
                np.concatenate([xTb[:, s0:s0 + T], xTb[:, :s0], xTb[:, s0 + T:]],
                               axis=1).reshape(NFC, 128, S))
        in_maps.append({
            "xtok": xt, "xT": xT_core,
            "wth": wth_h, "wph": wph_h, "wg": wg_h, "rw": rw_h,
            "gam": gam_h, "bet": bet_h,
            "wgu": wgu_h, "wd": wd_h, "sel": sel_h, "idn": idn_h,
        })
    res = run_bass_kernel_spmd(nc, in_maps, core_ids=list(range(NCORES)))
    outs = [np.asarray(res.results[c]["y"]).reshape(T, D) for c in range(NCORES)]
    full = np.concatenate(outs, axis=0).reshape(B, S, D)
    return full.astype(np.float32)


# version 26
# speedup vs baseline: 1.0173x; 1.0173x over previous
import numpy as np
import concourse.bass as bass
import concourse.mybir as mybir
from concourse.bass_utils import run_bass_kernel_spmd

F32 = mybir.dt.float32
BF16 = mybir.dt.bfloat16
AF = mybir.ActivationFunctionType
ALU = mybir.AluOpType
AX = mybir.AxisListType.X

D = 1024
NFC = D // 128
L = 256
NLC = L // 128
NE = 8
NJJ = 16
HIDSLOTS = 8
USE_CC = False
SCALE = 1.0 / 32.0
LN_EPS = 1e-5
ROUTER_EPS = 1e-8
CLAMP = 10000.0


class Builder:
    ENGS = ("sync", "gpsimd", "tensor", "vector", "scalar")
    OWN = {"tensor": "sT", "vector": "sV", "scalar": "sS"}

    def __init__(self, nc):
        self.nc = nc
        self.ops = {e: [] for e in self.ENGS}
        self.sems = {}
        self.cnt = {}
        self.last_wait = {}
        self.bank_free = {}
        self.buf_free = {}

    def sem(self, name, ctxs):
        h = self.nc.semaphore(name)
        ctxs.append(h)
        self.sems[name] = h.__enter__()
        self.cnt[name] = 0

    def op(self, eng, fn):
        self.ops[eng].append(fn)

    def wait(self, eng, sem, val):
        if val is None or val <= 0:
            return
        assert val <= self.cnt[sem], f"wait {eng} {sem}>={val} not yet emitted"
        key = (eng, sem)
        if self.last_wait.get(key, -1) >= val:
            return
        self.last_wait[key] = val
        h = self.sems[sem]
        self.op(eng, lambda e, h=h, v=val: e.wait_ge(h, v))

    def wpair(self, eng, pair):
        if pair is not None:
            self.wait(eng, pair[0], pair[1])

    def inc(self, sem, n=1):
        self.cnt[sem] += n
        return self.cnt[sem]

    def chain(self, eng, fns):
        sem = self.OWN[eng]
        h = self.sems[sem]
        last = None
        for f in fns:
            if last is not None:
                self.wait(eng, sem, last)
            self.op(eng, lambda e, f=f, h=h: f(e).then_inc(h, 1))
            last = self.inc(sem)
        return last

    def vbatch(self, fns):
        h = self.sems["sV"]
        for f in fns[:-1]:
            self.op("vector", f)
        self.op("vector", lambda v, f=fns[-1], h=h: f(v).then_inc(h, 1))
        return self.inc("sV")

    def op_inc(self, eng, fn):
        sem = self.OWN[eng]
        h = self.sems[sem]
        self.op(eng, lambda e, f=fn, h=h: f(e).then_inc(h, 1))
        return self.inc(sem)

    def emit_into(self, block):
        for eng in self.ENGS:
            if not self.ops[eng]:
                continue
            fns = self.ops[eng]
            getattr(block, eng)(lambda e, fns=fns: [f(e) for f in fns])
        self.ops = {e: [] for e in self.ENGS}


def build_program(T=2048, TB=8192, GRP=512, use_cc=USE_CC):
    if use_cc:
        TB = T
    NTC = T // 128
    TH = T // 2
    NTH = TH // 128
    SL = min(512, TH)
    NSL = TH // SL
    NGT = GRP // 128
    NGR = TB // GRP
    NOG = T // GRP
    GSL = min(512, GRP)
    NGS = GRP // GSL

    nc = bass.Bass("TRN2", target_bir_lowering=False)
    b = Builder(nc)
    ctxs = []

    def sb(name, shape, dt, clist):
        h = nc.sbuf_tensor(name, shape, dt)
        clist.append(h)
        return h.__enter__()

    xtok = nc.dram_tensor("xtok", [NTC, 128, D], F32, kind="ExternalInput")
    xT_d = nc.dram_tensor("xT", [NFC, 128, TB], F32, kind="ExternalInput")
    wth_d = nc.dram_tensor("wth", [NFC, 128, L], F32, kind="ExternalInput")
    wph_d = nc.dram_tensor("wph", [NFC, 128, L], F32, kind="ExternalInput")
    wg_d = nc.dram_tensor("wg", [NFC, 128, D], F32, kind="ExternalInput")
    rw_d = nc.dram_tensor("rw", [NFC, 128, NE], F32, kind="ExternalInput")
    gam_d = nc.dram_tensor("gam", [128, NFC], F32, kind="ExternalInput")
    bet_d = nc.dram_tensor("bet", [128, NFC], F32, kind="ExternalInput")
    wgu_d = nc.dram_tensor("wgu", [NE, NJJ, 2, NFC, 128, 128], F32, kind="ExternalInput")
    wd_d = nc.dram_tensor("wd", [NE, 2, HIDSLOTS, 128, D], F32, kind="ExternalInput")
    sel_d = nc.dram_tensor("sel", [NE, NE * 128], F32, kind="ExternalInput")
    idn_d = nc.dram_tensor("idn", [128, 128], F32, kind="ExternalInput")
    y_d = nc.dram_tensor("y", [NTC, 128, D], F32, kind="ExternalOutput")
    if use_cc:
        ccin_d = nc.dram_tensor("ccin", [NLC, 128, D], F32)
        ccout_d = nc.dram_tensor("ccout", [NLC, 128, D], F32)

    resid = sb("resid", [128, NTC * D], F32, ctxs)
    gub = [sb(f"gub{k}", [128, 2 * D], BF16, ctxs) for k in range(3)]
    wdb = [sb("wdb0", [128, HIDSLOTS * D], BF16, ctxs)]
    rw = sb("s_rw", [128, NE * NFC], BF16, ctxs)
    gam = sb("s_gam", [128, NFC], F32, ctxs)
    bet = sb("s_bet", [128, NFC], F32, ctxs)
    idn = sb("s_idn", [128, 128], F32, ctxs)
    sel = sb("s_sel", [NE, NE * 128], F32, ctxs)
    sums = sb("sums", [128, NTC], F32, ctxs)
    ssq = sb("ssq", [128, NTC], F32, ctxs)
    mu = sb("mu", [128, NTC], F32, ctxs)
    tmp16 = sb("tmp16", [128, NTC], F32, ctxs)
    rstd = sb("rstd", [128, NTC], F32, ctxs)
    ratio = sb("ratio", [128, NTC], F32, ctxs)
    mu1 = sb("mu1", [128, NTC], F32, ctxs)
    r2x = sb("r2x", [128, NTC], F32, ctxs)

    for s in ("sT", "sV", "sS", "wcon", "hcon", "hx", "out0", "out1",
              "xtg0", "xtg1", "gub0", "gub1", "gub2", "wdb0", "wth1",
              "ccup", "ccsm", "ccdn"):
        b.sem(s, ctxs)

    PSh = nc.psum_tensor("PS", [128, 4096], F32)
    ctxs.append(PSh)
    PS = PSh.__enter__()

    def bank(i, off=0, n=512):
        return PS[:, i * 512 + off:i * 512 + off + n]

    def gdma(sem, dst_ap, src_ap):
        b.op("gpsimd", lambda e, d=dst_ap, s=src_ap, h=b.sems[sem]:
             e.dma_start(d, s).then_inc(h, 16))
        return (sem, b.inc(sem, 16))

    def sdma(sem, dst_ap, src_ap, slow=False):
        b.op("sync", lambda e, d=dst_ap, s=src_ap, h=b.sems[sem], sl=slow:
             e.dma_start(d, s, allow_slow_non_contiguous=sl).then_inc(h, 16))
        return (sem, b.inc(sem, 16))

    def mm(out_ap, lh_ap, rh_ap, start, stop, inc=False):
        if inc:
            b.op("tensor", lambda t, o=out_ap, l=lh_ap, r=rh_ap, st=start, sp=stop,
                 h=b.sems["sT"]: t.matmul(o, l, r, start=st, stop=sp).then_inc(h, 1))
            return b.inc("sT")
        b.op("tensor", lambda t, o=out_ap, l=lh_ap, r=rh_ap, st=start, sp=stop:
             t.matmul(o, l, r, start=st, stop=sp))
        return None

    def vop(fn):
        b.op("vector", fn)

    def vop_inc(fn):
        b.op("vector", lambda v, f=fn, h=b.sems["sV"]: f(v).then_inc(h, 1))
        return b.inc("sV")

    def sop(fn):
        b.op("scalar", fn)

    def sop_inc(fn):
        b.op("scalar", lambda s, f=fn, h=b.sems["sS"]: f(s).then_inc(h, 1))
        return b.inc("sS")

    def reshape3(t_ap, c):
        return t_ap.rearrange("p (c n) -> p c n", c=c)


    def keepalive(points):
        if not points:
            return
        b.wpair("tensor", getattr(b, "v_hcon", None))
        n = len(points)
        for k, (sem, val) in enumerate(points):
            b.wait("tensor", sem, val)
            mm(bank(7, 0, 128), idn[0:1, 0:128], idn[0:1, 0:128],
               start=(k == 0), stop=(k == n - 1), inc=True)
        b.bank_free[7] = ("sT", b.cnt["sT"])

    actxs = []
    xtg = [sb(f"xtg{k}", [128, NFC * GRP], BF16, actxs) for k in range(2)]
    phg = sb("phg", [128, NGT * L], BF16, actxs)
    gg = sb("gg", [128, NGT * D], BF16, actxs)
    theta = sb("theta", [128, NLC * T], BF16, actxs)
    macc = sb("macc", [128, NLC * D], F32, actxs)
    msb = sb("msb", [128, NLC * D], BF16, actxs)
    wth = sb("s_wth", [128, NFC * L], BF16, actxs)
    wph = sb("s_wph", [128, NFC * L], BF16, actxs)
    wg = sb("s_wg", [128, NFC * D], BF16, actxs)

    v_wth = gdma("wth1", reshape3(wth[:, :], NFC),
                 wth_d[:, :, :].rearrange("c p n -> p c n"))
    sdma("hcon", gam[:, :], gam_d[:, :])
    sdma("hcon", bet[:, :], bet_d[:, :])
    sdma("hcon", idn[:, :], idn_d[:, :])
    v_hcon = sdma("hcon", sel[:, :], sel_d[:, :])
    b.v_hcon = v_hcon
    v_x = sdma("hx", reshape3(resid[:, :], NTC),
               xtok[:, :, :].rearrange("c p n -> p c n"))

    xtg_ready = [None, None]

    def load_group(g):
        k = g % 2
        b.wpair("gpsimd", b.buf_free.get(("xtg", k)))
        xtg_ready[k] = gdma(
            f"xtg{k}", reshape3(xtg[k][:, :], NFC),
            xT_d[:, :, g * GRP:(g + 1) * GRP].rearrange("c p n -> p c n"))

    load_group(0)
    gdma("wcon", reshape3(wph[:, :], NFC), wph_d[:, :, :].rearrange("c p n -> p c n"))
    gdma("wcon", reshape3(wg[:, :], NFC), wg_d[:, :, :].rearrange("c p n -> p c n"))
    v_wcon = gdma("wcon", reshape3(rw[:, :], NFC),
                  rw_d[:, :, :].rearrange("c p n -> p c n"))
    if NGR > 1:
        load_group(1)

    b.wpair("tensor", v_wth)
    mev_prev = None
    ev_prev_group = None
    for g in range(NGR):
        k = g % 2
        b.wpair("tensor", xtg_ready[k])
        own = g < NOG
        if own:
            for gs in range(NGS):
                b.wpair("tensor", b.bank_free.get(6))
                b.wpair("tensor", b.bank_free.get(7))
                for fc in range(NFC):
                    for lc in range(NLC):
                        mm(bank(6 + lc, 0, GSL),
                           wth[:, fc * L + lc * 128: fc * L + lc * 128 + 128],
                           xtg[k][:, fc * GRP + gs * GSL: fc * GRP + (gs + 1) * GSL],
                           start=(fc == 0), stop=(fc == NFC - 1),
                           inc=(fc == NFC - 1 and lc == NLC - 1))
                tmil = b.cnt["sT"]
                b.wait("vector", "sT", tmil)
                th_ev = b.vbatch([
                    (lambda v, c=lc * T + g * GRP + gs * GSL, lc=lc, n=GSL:
                     v.tensor_copy(theta[:, c:c + n], bank(6 + lc, 0, n)))
                    for lc in range(NLC)])
                b.bank_free[6] = ("sV", th_ev)
                b.bank_free[7] = ("sV", th_ev)
        b.wpair("tensor", v_wcon)
        if ev_prev_group is not None:
            b.wait("vector", "sV", ev_prev_group)
        ev = [None] * NGT
        ev_last = None
        for tci in range(NGT):
            bs = (tci % 2) * 3
            for bb in (bs, bs + 1, bs + 2):
                b.wpair("tensor", b.bank_free.get(bb))
            for fc in range(NFC):
                lh = xtg[k][:, fc * GRP + tci * 128: fc * GRP + tci * 128 + 128]
                st, sp = (fc == 0), (fc == NFC - 1)
                mm(bank(bs, 0, L), lh, wph[:, fc * L:(fc + 1) * L], st, sp)
                mm(bank(bs + 1), lh, wg[:, fc * D:fc * D + 512], st, sp)
                mm(bank(bs + 2), lh, wg[:, fc * D + 512:(fc + 1) * D], st, sp,
                   inc=sp)
            tmil = b.cnt["sT"]
            b.wait("vector", "sT", tmil)
            vop(lambda v, tci=tci, bs=bs:
                v.tensor_copy(phg[:, tci * L:(tci + 1) * L], bank(bs, 0, L)))
            vop(lambda v, tci=tci, bs=bs:
                v.tensor_copy(gg[:, tci * D:tci * D + 512], bank(bs + 1)))
            ev_last = vop_inc(lambda v, tci=tci, bs=bs:
                              v.tensor_copy(gg[:, tci * D + 512:(tci + 1) * D],
                                            bank(bs + 2)))
            ev[tci] = ev_last
            for bb in (bs, bs + 1, bs + 2):
                b.bank_free[bb] = ("sV", ev_last)
        ev_prev_group = ev_last
        b.buf_free[("xtg", k)] = ("sT", b.cnt["sT"])
        if g + 2 < NGR:
            load_group(g + 2)
        for lc in range(NLC):
            b.wpair("tensor", b.bank_free.get(6))
            b.wpair("tensor", b.bank_free.get(7))
            for tci in range(NGT):
                b.wait("tensor", "sV", ev[tci])
                for oh in range(2):
                    mm(bank(6 + oh),
                       phg[:, tci * L + lc * 128: tci * L + lc * 128 + 128],
                       gg[:, tci * D + oh * 512: tci * D + oh * 512 + 512],
                       start=(tci == 0), stop=(tci == NGT - 1),
                       inc=(tci == NGT - 1 and oh == 1))
            tmil = b.cnt["sT"]
            b.wait("vector", "sT", tmil)
            b.wait("vector", "sV", mev_prev)
            if g == 0:
                mev_prev = b.vbatch([
                    (lambda v, c=lc * D + oh * 512, bb=6 + oh:
                     v.tensor_copy(macc[:, c:c + 512], bank(bb)))
                    for oh in range(2)])
            else:
                mev_prev = b.vbatch([
                    (lambda v, c=lc * D + oh * 512, bb=6 + oh:
                     v.tensor_add(macc[:, c:c + 512], bank(bb), macc[:, c:c + 512]))
                    for oh in range(2)])
            b.bank_free[6] = ("sV", mev_prev)
            b.bank_free[7] = ("sV", mev_prev)

    if use_cc:
        b.wait("sync", "sV", mev_prev)
        v_ccup = sdma("ccup", ccin_d[:, :, :].rearrange("c p n -> p c n"),
                      reshape3(macc[:, :], NLC))
        b.wpair("gpsimd", v_ccup)
        b.op("gpsimd", lambda g, h=b.sems["ccsm"]: g.collective_compute(
            "AllReduce", mybir.AluOpType.add,
            replica_groups=[[0, 1, 2, 3], [4, 5, 6, 7]],
            ins=[ccin_d.ap().opt()], outs=[ccout_d.ap().opt()]).then_inc(h, 1))
        b.inc("ccsm")
        b.wait("gpsimd", "ccsm", 1)
        b.op("gpsimd", lambda g, h=b.sems["ccdn"]: g.dma_start(
            out=reshape3(macc[:, :], NLC),
            in_=ccout_d[:, :, :].rearrange("c p n -> p c n")).then_inc(h, 16))
        v_ccdn = ("ccdn", b.inc("ccdn", 16))
        b.wpair("scalar", v_ccdn)
        b.wpair("tensor", getattr(b, "v_hcon", None))
        b.wait("tensor", "sV", mev_prev)
        NSPIN = 60
        for k in range(NSPIN):
            mm(bank(7, 0, 128), idn[0:1, 0:128], idn[0:1, 0:128],
               start=(k == 0), stop=(k == NSPIN - 1), inc=(k == NSPIN - 1))
        b.bank_free[7] = ("sT", b.cnt["sT"])

    b.wait("scalar", "sV", mev_prev)
    v_msb = sop_inc(lambda s: s.activation(msb[:, :], macc[:, :], AF.Copy,
                                           bias=0.0, scale=float(SCALE)))

    b.wait("tensor", "sS", v_msb)
    b.wpair("vector", v_x)
    h1_done = None
    for tc in range(NTC):
        bs = (tc % 2) * 2
        b.wpair("tensor", b.bank_free.get(bs))
        b.wpair("tensor", b.bank_free.get(bs + 1))
        for oh in range(2):
            for lc in range(NLC):
                mm(bank(bs + oh),
                   theta[:, lc * T + tc * 128: lc * T + tc * 128 + 128],
                   msb[:, lc * D + oh * 512: lc * D + oh * 512 + 512],
                   start=(lc == 0), stop=(lc == NLC - 1),
                   inc=(lc == NLC - 1 and oh == 1))
        tmil = b.cnt["sT"]
        b.wait("vector", "sT", tmil)
        h1_done = b.vbatch([
            (lambda v, c=tc * D + oh * 512, bb=bs + oh:
             v.tensor_add(resid[:, c:c + 512], bank(bb), resid[:, c:c + 512]))
            for oh in range(2)])
        b.bank_free[bs] = ("sV", h1_done)
        b.bank_free[bs + 1] = ("sV", h1_done)

    def ln_stats(chunks, src, dump_sq, dump_sum, v_pre, t_pre, keep=False,
                 with_ratio=False):
        n = len(chunks)
        b.wait("scalar", "sV", v_pre)
        b.wait("scalar", "sT", t_pre)
        c0s = b.cnt["sS"]
        s_sq = b.chain("scalar", [
            (lambda s, a=src(tc), i=i, dq=dump_sq:
             s.activation(dq, a, AF.Square, accum_out=ssq[:, i:i + 1]))
            for i, tc in enumerate(chunks)])
        b.wait("vector", "sV", v_pre)
        b.wait("vector", "sT", t_pre)
        fns = [
            (lambda v, a=src(tc), i=i, ds=dump_sum:
             v.tensor_scalar(ds, a, 0.0, None, ALU.add, ALU.add,
                             accum_out=sums[:, i:i + 1]))
            for i, tc in enumerate(chunks)]
        fns += [
            lambda v: v.tensor_scalar_mul(mu[:, 0:n], sums[:, 0:n], 1.0 / D),
            lambda v: v.tensor_mul(tmp16[:, 0:n], mu[:, 0:n], mu[:, 0:n]),
        ]
        c0 = b.cnt["sV"]
        p1 = b.chain("vector", fns)
        if keep:
            pts = [("sS", v) for v in range(c0s + 1, s_sq + 1)]
            pts += [("sV", v) for v in range(c0 + 2, p1 + 1, 2)]
            keepalive(pts)
        b.wait("vector", "sV", p1)
        b.wait("vector", "sS", s_sq)
        v_var = b.chain("vector", [
            lambda v: v.scalar_tensor_tensor(tmp16[:, 0:n], ssq[:, 0:n], 1.0 / D,
                                             tmp16[:, 0:n], ALU.mult, ALU.subtract),
            lambda v: v.tensor_scalar_add(tmp16[:, 0:n], tmp16[:, 0:n], float(LN_EPS)),
            lambda v: v.reciprocal(rstd[:, 0:n], tmp16[:, 0:n]),
        ])
        if with_ratio:
            v_var = b.chain("vector", [
                lambda v: v.tensor_scalar(sums[:, 0:n], tmp16[:, 0:n],
                                          1.0 + float(LN_EPS), -float(LN_EPS),
                                          ALU.mult, ALU.add),
                lambda v: v.reciprocal(ssq[:, 0:n], sums[:, 0:n]),
                lambda v: v.tensor_mul(ratio[:, 0:n], tmp16[:, 0:n], ssq[:, 0:n]),
            ])
        b.wait("scalar", "sV", v_var)
        if with_ratio:
            sop(lambda s, n=n: s.activation(ratio[:, 0:n], ratio[:, 0:n], AF.Sqrt))
        s_r = sop_inc(lambda s, n=n: s.activation(rstd[:, 0:n], rstd[:, 0:n], AF.Sqrt))
        b.wait("vector", "sS", s_r)
        if with_ratio:
            b.chain("vector", [
                lambda v, n=n: v.tensor_mul(ratio[:, 0:n], ratio[:, 0:n],
                                            rstd[:, 0:n])])
        return s_r

    def layernorm(chunks, src, dst, dump_sq, dump_sum, v_pre, t_pre, keep=False,
                  with_ratio=False):
        s_r = ln_stats(chunks, src, dump_sq, dump_sum, v_pre, t_pre, keep=keep,
                       with_ratio=with_ratio)
        b.s_r_last = s_r
        return b.vbatch([
            (lambda v, a=src(tc), d=dst(tc), i=i:
             v.tensor_scalar(d, a, mu[:, i:i + 1], rstd[:, i:i + 1],
                             ALU.subtract, ALU.mult))
            for i, tc in enumerate(chunks)])

    s_r1 = ln_stats(
        list(range(NTC)),
        src=lambda tc: resid[:, tc * D:(tc + 1) * D],
        dump_sq=gg[:, 0:D], dump_sum=gg[:, D:2 * D],
        v_pre=h1_done, t_pre=None, keep=True, with_ratio=True)
    b.s_r_last = s_r1
    v_zsc = b.chain("vector", [
        lambda v: v.tensor_copy(mu1[:, 0:NTC], mu[:, 0:NTC]),
        lambda v: v.tensor_scalar_mul(r2x[:, 0:NTC], rstd[:, 0:NTC], 2.0),
    ])
    b.v_zsc = v_zsc

    NU = 2 * NE * NJJ
    NW = 2 * NE * 2

    def unit_src(n):
        e = (n // NJJ) % NE
        jj = n % NJJ
        return wgu_d[e, jj, :, :, :, :].rearrange("a c p n -> p a c n")

    def wd_src(m):
        e = (m // 2) % NE
        p = m % 2
        return wd_d[e, p, :, :, :].rearrange("c p n -> p c n")

    gu_land = {}
    wd_land = {}

    def load_gu(n):
        k = n % 3
        b.wpair("gpsimd", b.buf_free.get(("gub", k)))
        gu_land[n] = gdma(
            f"gub{k}", gub[k][:, :].rearrange("p (a c n) -> p a c n", a=2, c=NFC),
            unit_src(n))

    def load_wd(m):
        b.wpair("gpsimd", b.buf_free.get(("wdb",)))
        wd_land[m] = gdma("wdb0", reshape3(wdb[0][:, :], HIDSLOTS), wd_src(m))

    load_gu(0)
    load_gu(1)
    load_gu(2)
    load_wd(0)

    blkA = nc.Block()
    blkA_g = blkA.__enter__()
    b.emit_into(blkA_g)
    blkA.__exit__(None, None, None)

    for c in reversed(actxs):
        c.__exit__(None, None, None)

    bctxs = []
    xnT = sb("xnT", [128, NFC * TH], BF16, bctxs)
    xe = sb("xe", [128, NFC * TH], BF16, bctxs)
    hid = sb("hid", [128, 2 * HIDSLOTS * TH], BF16, bctxs)
    acc = sb("acc", [128, NTH * D], F32, bctxs)
    z2 = sb("z2", [128, D], F32, bctxs)
    srep = sb("srep", [128, TH], BF16, bctxs)
    sg = [sb("sg0", [128, TH], BF16, bctxs)]
    logits = sb("logits", [128, NTH * NE], F32, bctxs)
    probs = sb("probs", [128, NTH * NE], F32, bctxs)
    mk = sb("mk", [128, NTH * NE], F32, bctxs)
    gates = mk
    gatesT = sb("gatesT", [NE, TH], F32, bctxs)
    rst8 = sums
    rst8b = ssq

    unit_ctr = 0
    wd_ctr = 0
    out_dma = [None, None]
    hid_ready = {}
    last_down_mil = None

    for h in range(2):
        hc0 = h * NTH
        b.wpair("scalar", v_hcon)
        b.wpair("vector", v_hcon)
        b.wait("vector", "sV", b.v_zsc)
        b.wait("vector", "sS", b.s_r_last)

        xnT_done = None
        for i in range(NTH):
            tc = hc0 + i
            zs = 0
            b.wpair("vector", b.buf_free.get(("z", zs)))
            v_z = vop_inc(lambda v, tc=tc, zs=zs:
                          v.tensor_scalar(z2[:, zs * D:(zs + 1) * D],
                                          resid[:, tc * D:(tc + 1) * D],
                                          mu1[:, tc:tc + 1], ratio[:, tc:tc + 1],
                                          ALU.subtract, ALU.mult))
            keepalive([("sV", v_z)])
            b.wpair("tensor", v_hcon)
            bs = (i % 2) * 2
            b.wpair("tensor", b.bank_free.get(bs))
            b.wpair("tensor", b.bank_free.get(bs + 1))
            for fc in range(NFC):
                bb = bs + fc // 4
                off = (fc % 4) * 128
                if fc < NFC - 1:
                    b.op("tensor", lambda t, zs=zs, fc=fc, bb=bb, off=off:
                         t.transpose(bank(bb, off, 128),
                                     z2[:, zs * D + fc * 128: zs * D + (fc + 1) * 128],
                                     idn[:, :]))
                else:
                    tmil = b.op_inc("tensor", lambda t, zs=zs, fc=fc, bb=bb, off=off:
                                    t.transpose(bank(bb, off, 128),
                                                z2[:, zs * D + fc * 128: zs * D + (fc + 1) * 128],
                                                idn[:, :]))
            b.buf_free[("z", zs)] = ("sT", tmil)
            b.wait("scalar", "sT", tmil)
            b.wpair("scalar", v_hcon)
            for fc in range(NFC):
                bb = bs + fc // 4
                off = (fc % 4) * 128
                if fc < NFC - 1:
                    sop(lambda s, fc=fc, bb=bb, off=off, i=i: s.activation(
                        xnT[:, fc * TH + i * 128: fc * TH + (i + 1) * 128],
                        bank(bb, off, 128), AF.Identity,
                        bias=bet[:, fc:fc + 1], scale=gam[:, fc:fc + 1]))
                else:
                    xnT_done = b.op_inc("scalar", lambda s, fc=fc, bb=bb, off=off, i=i:
                                        s.activation(
                        xnT[:, fc * TH + i * 128: fc * TH + (i + 1) * 128],
                        bank(bb, off, 128), AF.Identity,
                        bias=bet[:, fc:fc + 1], scale=gam[:, fc:fc + 1]))
            b.bank_free[bs] = ("sS", xnT_done)
            b.bank_free[bs + 1] = ("sS", xnT_done)
            b.wait("tensor", "sS", xnT_done)
            b.wpair("tensor", v_wcon)
            b.wpair("tensor", b.bank_free.get(4))
            for fc in range(NFC):
                mm(bank(4, 0, NE), xnT[:, fc * TH + i * 128: fc * TH + (i + 1) * 128],
                   rw[:, fc * NE:(fc + 1) * NE],
                   start=(fc == 0), stop=(fc == NFC - 1), inc=(fc == NFC - 1))
            tmil = b.cnt["sT"]
            b.wait("vector", "sT", tmil)
            v_lg = vop_inc(lambda v, i=i: v.tensor_copy(
                logits[:, i * NE:(i + 1) * NE], bank(4, 0, NE)))
            b.bank_free[4] = ("sV", v_lg)


        lg3 = reshape3(logits[:, 0:NTH * NE], NTH)
        pr3 = reshape3(probs[:, 0:NTH * NE], NTH)
        mk3 = reshape3(mk[:, 0:NTH * NE], NTH)

        def bc(stat):
            s3 = stat[:, 0:NTH].rearrange("p (c o) -> p c o", o=1)
            _, out = bass.broadcast_tensor_aps(lg3, s3)
            return out

        b.wait("vector", "sV", v_lg)
        c0r = b.cnt["sV"]
        v_sub = b.chain("vector", [
            lambda v: v.tensor_scalar(logits[:, 0:NTH * NE], logits[:, 0:NTH * NE],
                                      float(CLAMP), float(-CLAMP), ALU.min, ALU.max),
            lambda v: v.tensor_reduce(
                rst8[:, 0:NTH].rearrange("p (c o) -> p c o", o=1), lg3, AX, ALU.max),
            lambda v: v.tensor_sub(lg3, lg3, bc(rst8)),
        ])
        b.wait("scalar", "sV", v_sub)
        s_exp = sop_inc(lambda s: s.activation(logits[:, 0:NTH * NE],
                                               logits[:, 0:NTH * NE], AF.Exp))
        b.wait("vector", "sS", s_exp)
        v_gt = b.chain("vector", [
            lambda v: v.tensor_reduce(
                rst8[:, 0:NTH].rearrange("p (c o) -> p c o", o=1), lg3, AX, ALU.add),
            lambda v: v.tensor_scalar_add(rst8[:, 0:NTH], rst8[:, 0:NTH],
                                          float(ROUTER_EPS)),
            lambda v: v.reciprocal(rst8b[:, 0:NTH], rst8[:, 0:NTH]),
            lambda v: v.tensor_mul(pr3, lg3, bc(rst8b)),
            lambda v: v.tensor_scalar(probs[:, 0:NTH * NE], probs[:, 0:NTH * NE],
                                      float(ROUTER_EPS), 1.0, ALU.max, ALU.min),
            lambda v: v.tensor_reduce(
                rst8[:, 0:NTH].rearrange("p (c o) -> p c o", o=1), pr3, AX, ALU.max),
            lambda v: v.tensor_tensor(mk3, pr3, bc(rst8), ALU.is_ge),
            lambda v: v.scalar_tensor_tensor(mk[:, 0:NTH * NE], mk[:, 0:NTH * NE],
                                             -1e30, probs[:, 0:NTH * NE],
                                             ALU.mult, ALU.add),
            lambda v: v.tensor_reduce(
                rst8b[:, 0:NTH].rearrange("p (c o) -> p c o", o=1), mk3, AX, ALU.max),
            lambda v: v.tensor_tensor(mk3, pr3, bc(rst8b), ALU.is_ge),
            lambda v: v.tensor_mul(gates[:, 0:NTH * NE],
                                   probs[:, 0:NTH * NE], mk[:, 0:NTH * NE]),
        ])
        keepalive([("sV", v) for v in range(c0r + 2, v_gt + 1, 3)])

        b.wait("tensor", "sV", v_gt)
        b.wpair("tensor", b.bank_free.get(4))
        b.wpair("tensor", b.bank_free.get(5))
        tmil = None
        for i in range(NTH):
            bb = 4 + i // 4
            off = (i % 4) * 128
            if i < NTH - 1:
                b.op("tensor", lambda t, i=i, bb=bb, off=off:
                     t.transpose(PS[0:NE, bb * 512 + off:bb * 512 + off + 128],
                                 gates[:, i * NE:(i + 1) * NE], idn[:, :]))
            else:
                tmil = b.op_inc("tensor", lambda t, i=i, bb=bb, off=off:
                                t.transpose(PS[0:NE, bb * 512 + off:bb * 512 + off + 128],
                                            gates[:, i * NE:(i + 1) * NE], idn[:, :]))
        b.wait("vector", "sT", tmil)
        b.wait("vector", "sV", v_gt)
        gT_done = b.vbatch([
            (lambda v, bb=bb, w=min(512, NTH * 128 - (bb - 4) * 512):
             v.tensor_copy(gatesT[:, (bb - 4) * 512:(bb - 4) * 512 + w],
                           PS[0:NE, bb * 512:bb * 512 + w]))
            for bb in range(4, 4 + (NTH * 128 + 511) // 512)])
        b.bank_free[4] = ("sV", gT_done)
        b.bank_free[5] = ("sV", gT_done)

        def emit_srep_xe(e, banks):
            b.wait("tensor", "sV", gT_done)
            b.wpair("tensor", v_hcon)
            for slc in range(NSL):
                b.wpair("tensor", b.bank_free.get(banks[slc]))
                mm(bank(banks[slc], 0, SL), sel[:, e * 128:(e + 1) * 128],
                   gatesT[:, slc * SL:(slc + 1) * SL], start=True, stop=True,
                   inc=(slc == NSL - 1))
            tmil = b.cnt["sT"]
            b.wait("vector", "sT", tmil)
            b.wpair("vector", b.buf_free.get(("srep",)))
            v_srep = b.vbatch([
                (lambda v, slc=slc, bk=banks[slc]: v.tensor_copy(
                    srep[:, slc * SL:(slc + 1) * SL], bank(bk, 0, SL)))
                for slc in range(NSL)])
            for slc in range(NSL):
                b.bank_free[banks[slc]] = ("sV", v_srep)
            b.wpair("vector", b.buf_free.get(("xe",)))
            b.wait("vector", "sS", xnT_done)
            b.wait("vector", "sV", v_srep)
            c0x = b.cnt["sV"]
            v_xe = b.chain("vector", [
                (lambda v, fc=fc: v.tensor_mul(
                    xe[:, fc * TH:(fc + 1) * TH], xnT[:, fc * TH:(fc + 1) * TH],
                    srep[:, 0:TH]))
                for fc in range(NFC)])
            if e == 0:
                keepalive([("sV", v) for v in range(c0x + 2, v_xe + 1, 2)])
            b.buf_free[("srep",)] = ("sV", v_xe)
            return v_xe

        v_xe = emit_srep_xe(0, banks=(2, 3))

        def gu_pass(p, e):
            nonlocal unit_ctr
            hset = (p % 2) * HIDSLOTS
            for j2 in range(HIDSLOTS):
                uk = unit_ctr % 3
                b.wpair("tensor", gu_land[unit_ctr])
                b.wpair("tensor", b.buf_free.get(("hid", p % 2, j2)))
                gb = 4 if j2 % 2 == 0 else 0
                ub = 6 if j2 % 2 == 0 else 2
                mils = {}
                for a, bs in ((0, gb), (1, ub)):
                    for slc in range(NSL):
                        b.wpair("tensor", b.bank_free.get(bs + slc))
                    for fc in range(NFC):
                        for slc in range(NSL):
                            mm(bank(bs + slc, 0, SL),
                               gub[uk][:, a * D + fc * 128:a * D + (fc + 1) * 128],
                               xe[:, fc * TH + slc * SL: fc * TH + (slc + 1) * SL],
                               start=(fc == 0), stop=(fc == NFC - 1),
                               inc=(fc == NFC - 1 and slc == NSL - 1))
                    mils[a] = b.cnt["sT"]
                b.buf_free[("gub", uk)] = ("sT", mils[1])
                if unit_ctr + 3 < NU:
                    load_gu(unit_ctr + 3)
                sgk = sg[0]
                b.wait("scalar", "sT", mils[0])
                b.wpair("scalar", b.buf_free.get(("sg",)))
                for slc in range(NSL):
                    if slc < NSL - 1:
                        sop(lambda s, slc=slc, sgk=sgk, gb=gb: s.activation(
                            sgk[:, slc * SL:(slc + 1) * SL], bank(gb + slc, 0, SL),
                            AF.Silu))
                    else:
                        s_silu = b.op_inc("scalar", lambda s, slc=slc, sgk=sgk, gb=gb:
                                          s.activation(sgk[:, slc * SL:(slc + 1) * SL],
                                                       bank(gb + slc, 0, SL), AF.Silu))
                for slc in range(NSL):
                    b.bank_free[gb + slc] = ("sS", s_silu)
                b.wait("vector", "sT", mils[1])
                b.wait("vector", "sS", s_silu)
                v_hid = vop_inc(lambda v, c=(hset + j2) * TH, sgk=sgk, ub=ub:
                                v.tensor_mul(hid[:, c:c + TH], sgk[:, 0:TH],
                                             PS[:, ub * 512: ub * 512 + TH]))
                hid_ready[(p % 2, j2)] = v_hid
                b.buf_free[("sg",)] = ("sV", v_hid)
                for slc in range(NSL):
                    b.bank_free[ub + slc] = ("sV", v_hid)
                unit_ctr += 1
            return b.cnt["sT"]

        def down_pass(p, first, srep_fn=None):
            nonlocal wd_ctr, acc_ev, last_down_mil
            hset = (p % 2) * HIDSLOTS
            b.wpair("tensor", wd_land[wd_ctr])
            b.wpair("vector", b.buf_free.get(("accw",)))
            pairs = (4, 6, 0)
            srep_at = min(2, NTH - 1)
            for i in range(NTH):
                if i == srep_at and srep_fn is not None:
                    srep_fn()
                bs = pairs[i % 3]
                b.wpair("tensor", b.bank_free.get(bs))
                b.wpair("tensor", b.bank_free.get(bs + 1))
                for ffc in range(HIDSLOTS):
                    b.wait("tensor", "sV", hid_ready[(p % 2, ffc)])
                    for oh in range(2):
                        mm(bank(bs + oh),
                           hid[:, (hset + ffc) * TH + i * 128:
                               (hset + ffc) * TH + (i + 1) * 128],
                           wdb[0][:, ffc * D + oh * 512: ffc * D + oh * 512 + 512],
                           start=(ffc == 0), stop=(ffc == HIDSLOTS - 1),
                           inc=(ffc == HIDSLOTS - 1 and oh == 1))
                tmil = b.cnt["sT"]
                b.wait("vector", "sT", tmil)
                if h == 1 and first and i == 0:
                    b.wpair("vector", out_dma[0])
                if first:
                    acc_ev = b.vbatch([
                        (lambda v, c=i * D + oh * 512, bb=bs + oh:
                         v.tensor_copy(acc[:, c:c + 512], bank(bb)))
                        for oh in range(2)])
                else:
                    acc_ev = b.vbatch([
                        (lambda v, c=i * D + oh * 512, bb=bs + oh:
                         v.tensor_add(acc[:, c:c + 512], bank(bb),
                                      acc[:, c:c + 512]))
                        for oh in range(2)])
                b.bank_free[bs] = ("sV", acc_ev)
                b.bank_free[bs + 1] = ("sV", acc_ev)
            b.buf_free[("accw",)] = ("sV", acc_ev)
            last_down_mil = b.cnt["sT"]
            for j2 in range(HIDSLOTS):
                b.buf_free[("hid", p % 2, j2)] = ("sT", last_down_mil)
            b.buf_free[("wdb",)] = ("sT", last_down_mil)
            if wd_ctr + 1 < NW:
                load_wd(wd_ctr + 1)
            wd_ctr += 1

        acc_ev = None
        pending = None
        for p in range(2 * NE):
            e, hp = p // 2, p % 2
            if hp == 0:
                b.wait("tensor", "sV", v_xe)
            gu_last = gu_pass(p, e)
            srep_fn = None
            if hp == 1 and e + 1 < NE:
                b.buf_free[("xe",)] = ("sT", gu_last)

                def srep_fn(e=e):
                    nonlocal v_xe
                    v_xe = emit_srep_xe(e + 1, banks=(2, 3))
            if pending is not None:
                down_pass(pending, first=(pending == 0), srep_fn=srep_fn)
            pending = p
        down_pass(pending, first=False, srep_fn=None)

        b.wait("vector", "sV", acc_ev)
        fns_h2 = []
        for i in range(NTH):
            tc = hc0 + i
            fns_h2.append(lambda v, tc=tc: v.tensor_scalar(
                resid[:, tc * D:(tc + 1) * D], resid[:, tc * D:(tc + 1) * D],
                mu1[:, tc:tc + 1], r2x[:, tc:tc + 1], ALU.subtract, ALU.mult))
            fns_h2.append(lambda v, tc=tc, i=i: v.tensor_add(
                resid[:, tc * D:(tc + 1) * D], resid[:, tc * D:(tc + 1) * D],
                acc[:, i * D:(i + 1) * D]))
        h2_done = b.chain("vector", fns_h2)
        ln3_done = layernorm(
            list(range(hc0, hc0 + NTH)),
            src=lambda tc: resid[:, tc * D:(tc + 1) * D],
            dst=lambda tc: acc[:, (tc - hc0) * D:(tc - hc0 + 1) * D],
            dump_sq=hid[:, 0:D], dump_sum=z2[:, 0:D],
            v_pre=h2_done, t_pre=last_down_mil, keep=(h == 0))
        b.wait("sync", "sV", ln3_done)
        out_dma[h] = sdma(f"out{h}", y_d[hc0:hc0 + NTH, :, :].rearrange("c p n -> p c n"),
                          reshape3(acc[:, 0:NTH * D], NTH))

    b.wpair("sync", out_dma[0])
    b.wpair("sync", out_dma[1])

    blkB = nc.Block()
    blkB_g = blkB.__enter__()
    b.emit_into(blkB_g)
    blkB.__exit__(None, None, None)

    for c in reversed(bctxs):
        c.__exit__(None, None, None)
    for c in reversed(ctxs):
        c.__exit__(None, None, None)
    return nc


def kernel(L0_attn_x, L0_attn_W_theta, L0_attn_W_phi, L0_attn_W_g,
           L0_ffn_ln_gamma, L0_ffn_ln_beta, L0_ffn_router_w,
           L0_ffn_gate_up_w, L0_ffn_down_w):
    x = np.ascontiguousarray(np.asarray(L0_attn_x, dtype=np.float32))
    B, S, _ = x.shape
    NCORES = 8
    CPB = NCORES // B
    T = S // CPB

    def wT_chunks(w):
        return np.ascontiguousarray(np.asarray(w, np.float32).T.reshape(NFC, 128, -1))

    wth_h = wT_chunks(L0_attn_W_theta)
    wph_h = wT_chunks(L0_attn_W_phi)
    wg_h = wT_chunks(L0_attn_W_g)
    rw_h = wT_chunks(L0_ffn_router_w)
    gam_h = np.ascontiguousarray(np.asarray(L0_ffn_ln_gamma, np.float32).reshape(NFC, 128).T)
    bet_h = np.ascontiguousarray(np.asarray(L0_ffn_ln_beta, np.float32).reshape(NFC, 128).T)
    wgu = np.asarray(L0_ffn_gate_up_w, np.float32)
    wgu_h = np.ascontiguousarray(
        wgu.reshape(NE, 2, NJJ, 128, D).transpose(0, 2, 1, 4, 3)
           .reshape(NE, NJJ, 2, NFC, 128, 128))
    wd = np.asarray(L0_ffn_down_w, np.float32)
    wd_h = np.ascontiguousarray(
        wd.transpose(0, 2, 1).reshape(NE, 2, HIDSLOTS, 128, D))
    sel_h = np.zeros((NE, NE * 128), np.float32)
    for e in range(NE):
        sel_h[e, e * 128:(e + 1) * 128] = 1.0
    idn_h = np.eye(128, dtype=np.float32)

    xT_batches = [np.ascontiguousarray(x[bb].T) for bb in range(B)]

    nc = build_program(T=T, TB=S, GRP=1024)
    in_maps = []
    for c in range(NCORES):
        bb = c // CPB
        s0 = (c % CPB) * T
        xt = np.ascontiguousarray(x[bb, s0:s0 + T].reshape(T // 128, 128, D))
        xTb = xT_batches[bb]
        if USE_CC:
            xT_core = np.ascontiguousarray(xTb[:, s0:s0 + T].reshape(NFC, 128, T))
        else:
            xT_core = np.ascontiguousarray(
                np.concatenate([xTb[:, s0:s0 + T], xTb[:, :s0], xTb[:, s0 + T:]],
                               axis=1).reshape(NFC, 128, S))
        in_maps.append({
            "xtok": xt, "xT": xT_core,
            "wth": wth_h, "wph": wph_h, "wg": wg_h, "rw": rw_h,
            "gam": gam_h, "bet": bet_h,
            "wgu": wgu_h, "wd": wd_h, "sel": sel_h, "idn": idn_h,
        })
    res = run_bass_kernel_spmd(nc, in_maps, core_ids=list(range(NCORES)))
    outs = [np.asarray(res.results[c]["y"]).reshape(T, D) for c in range(NCORES)]
    full = np.concatenate(outs, axis=0).reshape(B, S, D)
    return full.astype(np.float32)


# version 27
# speedup vs baseline: 1.0334x; 1.0158x over previous
import numpy as np
import concourse.bass as bass
import concourse.mybir as mybir
from concourse.bass_utils import run_bass_kernel_spmd

F32 = mybir.dt.float32
BF16 = mybir.dt.bfloat16
AF = mybir.ActivationFunctionType
ALU = mybir.AluOpType
AX = mybir.AxisListType.X

D = 1024
NFC = D // 128
L = 256
NLC = L // 128
NE = 8
NJJ = 16
HIDSLOTS = 8
USE_CC = False
SCALE = 1.0 / 32.0
LN_EPS = 1e-5
ROUTER_EPS = 1e-8
CLAMP = 10000.0


class Builder:
    ENGS = ("sync", "gpsimd", "tensor", "vector", "scalar")
    OWN = {"tensor": "sT", "vector": "sV", "scalar": "sS"}

    def __init__(self, nc):
        self.nc = nc
        self.ops = {e: [] for e in self.ENGS}
        self.sems = {}
        self.cnt = {}
        self.last_wait = {}
        self.bank_free = {}
        self.buf_free = {}

    def sem(self, name, ctxs):
        h = self.nc.semaphore(name)
        ctxs.append(h)
        self.sems[name] = h.__enter__()
        self.cnt[name] = 0

    def op(self, eng, fn):
        self.ops[eng].append(fn)

    def wait(self, eng, sem, val):
        if val is None or val <= 0:
            return
        assert val <= self.cnt[sem], f"wait {eng} {sem}>={val} not yet emitted"
        key = (eng, sem)
        if self.last_wait.get(key, -1) >= val:
            return
        self.last_wait[key] = val
        h = self.sems[sem]
        self.op(eng, lambda e, h=h, v=val: e.wait_ge(h, v))

    def wpair(self, eng, pair):
        if pair is not None:
            self.wait(eng, pair[0], pair[1])

    def inc(self, sem, n=1):
        self.cnt[sem] += n
        return self.cnt[sem]

    def chain(self, eng, fns):
        sem = self.OWN[eng]
        h = self.sems[sem]
        last = None
        for f in fns:
            if last is not None:
                self.wait(eng, sem, last)
            self.op(eng, lambda e, f=f, h=h: f(e).then_inc(h, 1))
            last = self.inc(sem)
        return last

    def vbatch(self, fns):
        h = self.sems["sV"]
        for f in fns[:-1]:
            self.op("vector", f)
        self.op("vector", lambda v, f=fns[-1], h=h: f(v).then_inc(h, 1))
        return self.inc("sV")

    def op_inc(self, eng, fn):
        sem = self.OWN[eng]
        h = self.sems[sem]
        self.op(eng, lambda e, f=fn, h=h: f(e).then_inc(h, 1))
        return self.inc(sem)

    def emit_into(self, block):
        for eng in self.ENGS:
            if not self.ops[eng]:
                continue
            fns = self.ops[eng]
            getattr(block, eng)(lambda e, fns=fns: [f(e) for f in fns])
        self.ops = {e: [] for e in self.ENGS}


def build_program(T=2048, TB=8192, GRP=512, use_cc=USE_CC):
    if use_cc:
        TB = T
    NTC = T // 128
    TH = T // 2
    NTH = TH // 128
    SL = min(512, TH)
    NSL = TH // SL
    NGT = GRP // 128
    NGR = TB // GRP
    NOG = T // GRP
    GSL = min(512, GRP)
    NGS = GRP // GSL

    nc = bass.Bass("TRN2", target_bir_lowering=False)
    b = Builder(nc)
    ctxs = []

    def sb(name, shape, dt, clist):
        h = nc.sbuf_tensor(name, shape, dt)
        clist.append(h)
        return h.__enter__()

    xtok = nc.dram_tensor("xtok", [NTC, 128, D], F32, kind="ExternalInput")
    xT_d = nc.dram_tensor("xT", [NFC, 128, TB], F32, kind="ExternalInput")
    wth_d = nc.dram_tensor("wth", [NFC, 128, L], F32, kind="ExternalInput")
    wph_d = nc.dram_tensor("wph", [NFC, 128, L], F32, kind="ExternalInput")
    wg_d = nc.dram_tensor("wg", [NFC, 128, D], F32, kind="ExternalInput")
    rw_d = nc.dram_tensor("rw", [NFC, 128, NE], F32, kind="ExternalInput")
    gam_d = nc.dram_tensor("gam", [128, NFC], F32, kind="ExternalInput")
    bet_d = nc.dram_tensor("bet", [128, NFC], F32, kind="ExternalInput")
    wgu_d = nc.dram_tensor("wgu", [NE, NJJ, 2, NFC, 128, 128], F32, kind="ExternalInput")
    wd_d = nc.dram_tensor("wd", [NE, 2, HIDSLOTS, 128, D], F32, kind="ExternalInput")
    sel_d = nc.dram_tensor("sel", [NE, NE * 128], F32, kind="ExternalInput")
    idn_d = nc.dram_tensor("idn", [128, 128], F32, kind="ExternalInput")
    y_d = nc.dram_tensor("y", [NTC, 128, D], F32, kind="ExternalOutput")
    if use_cc:
        ccin_d = nc.dram_tensor("ccin", [NLC, 128, D], F32)
        ccout_d = nc.dram_tensor("ccout", [NLC, 128, D], F32)

    resid = sb("resid", [128, NTC * D], F32, ctxs)
    gub = [sb(f"gub{k}", [128, 2 * D], BF16, ctxs) for k in range(3)]
    wdb = [sb("wdb0", [128, HIDSLOTS * D], BF16, ctxs)]
    rw = sb("s_rw", [128, NE * NFC], BF16, ctxs)
    gam = sb("s_gam", [128, NFC], F32, ctxs)
    bet = sb("s_bet", [128, NFC], F32, ctxs)
    idn = sb("s_idn", [128, 128], F32, ctxs)
    sel = sb("s_sel", [NE, NE * 128], F32, ctxs)
    sums = sb("sums", [128, NTC], F32, ctxs)
    ssq = sb("ssq", [128, NTC], F32, ctxs)
    mu = sb("mu", [128, NTC], F32, ctxs)
    tmp16 = sb("tmp16", [128, NTC], F32, ctxs)
    rstd = sb("rstd", [128, NTC], F32, ctxs)
    ratio = sb("ratio", [128, NTC], F32, ctxs)
    mu1 = sb("mu1", [128, NTC], F32, ctxs)
    r2x = sb("r2x", [128, NTC], F32, ctxs)

    for s in ("sT", "sV", "sS", "wcon", "hcon", "hx", "out0", "out1",
              "xtg0", "xtg1", "gub0", "gub1", "gub2", "wdb0", "wth1",
              "ccup", "ccsm", "ccdn"):
        b.sem(s, ctxs)

    PSh = nc.psum_tensor("PS", [128, 4096], F32)
    ctxs.append(PSh)
    PS = PSh.__enter__()

    def bank(i, off=0, n=512):
        return PS[:, i * 512 + off:i * 512 + off + n]

    def gdma(sem, dst_ap, src_ap):
        b.op("gpsimd", lambda e, d=dst_ap, s=src_ap, h=b.sems[sem]:
             e.dma_start(d, s).then_inc(h, 16))
        return (sem, b.inc(sem, 16))

    def sdma(sem, dst_ap, src_ap, slow=False):
        b.op("sync", lambda e, d=dst_ap, s=src_ap, h=b.sems[sem], sl=slow:
             e.dma_start(d, s, allow_slow_non_contiguous=sl).then_inc(h, 16))
        return (sem, b.inc(sem, 16))

    def mm(out_ap, lh_ap, rh_ap, start, stop, inc=False):
        if inc:
            b.op("tensor", lambda t, o=out_ap, l=lh_ap, r=rh_ap, st=start, sp=stop,
                 h=b.sems["sT"]: t.matmul(o, l, r, start=st, stop=sp).then_inc(h, 1))
            return b.inc("sT")
        b.op("tensor", lambda t, o=out_ap, l=lh_ap, r=rh_ap, st=start, sp=stop:
             t.matmul(o, l, r, start=st, stop=sp))
        return None

    def vop(fn):
        b.op("vector", fn)

    def vop_inc(fn):
        b.op("vector", lambda v, f=fn, h=b.sems["sV"]: f(v).then_inc(h, 1))
        return b.inc("sV")

    def sop(fn):
        b.op("scalar", fn)

    def sop_inc(fn):
        b.op("scalar", lambda s, f=fn, h=b.sems["sS"]: f(s).then_inc(h, 1))
        return b.inc("sS")

    def reshape3(t_ap, c):
        return t_ap.rearrange("p (c n) -> p c n", c=c)


    def keepalive(points):
        if not points:
            return
        b.wpair("tensor", getattr(b, "v_hcon", None))
        n = len(points)
        for k, (sem, val) in enumerate(points):
            b.wait("tensor", sem, val)
            mm(bank(7, 0, 128), idn[0:1, 0:128], idn[0:1, 0:128],
               start=(k == 0), stop=(k == n - 1), inc=True)
        b.bank_free[7] = ("sT", b.cnt["sT"])

    actxs = []
    xtg = [sb(f"xtg{k}", [128, NFC * GRP], BF16, actxs) for k in range(2)]
    phg = sb("phg", [128, NGT * L], BF16, actxs)
    gg = sb("gg", [128, NGT * D], BF16, actxs)
    theta = sb("theta", [128, NLC * T], BF16, actxs)
    macc = sb("macc", [128, NLC * D], F32, actxs)
    msb = sb("msb", [128, NLC * D], BF16, actxs)
    wth = sb("s_wth", [128, NFC * L], BF16, actxs)
    wph = sb("s_wph", [128, NFC * L], BF16, actxs)
    wg = sb("s_wg", [128, NFC * D], BF16, actxs)

    v_wth = gdma("wth1", reshape3(wth[:, :], NFC),
                 wth_d[:, :, :].rearrange("c p n -> p c n"))
    sdma("hcon", gam[:, :], gam_d[:, :])
    sdma("hcon", bet[:, :], bet_d[:, :])
    sdma("hcon", idn[:, :], idn_d[:, :])
    v_hcon = sdma("hcon", sel[:, :], sel_d[:, :])
    b.v_hcon = v_hcon
    v_x = sdma("hx", reshape3(resid[:, :], NTC),
               xtok[:, :, :].rearrange("c p n -> p c n"))

    xtg_ready = [None, None]

    def load_group(g):
        k = g % 2
        b.wpair("gpsimd", b.buf_free.get(("xtg", k)))
        xtg_ready[k] = gdma(
            f"xtg{k}", reshape3(xtg[k][:, :], NFC),
            xT_d[:, :, g * GRP:(g + 1) * GRP].rearrange("c p n -> p c n"))

    load_group(0)
    gdma("wcon", reshape3(wph[:, :], NFC), wph_d[:, :, :].rearrange("c p n -> p c n"))
    gdma("wcon", reshape3(wg[:, :], NFC), wg_d[:, :, :].rearrange("c p n -> p c n"))
    v_wcon = gdma("wcon", reshape3(rw[:, :], NFC),
                  rw_d[:, :, :].rearrange("c p n -> p c n"))
    if NGR > 1:
        load_group(1)

    b.wpair("tensor", v_wth)
    mev_prev = None
    ev_prev_group = None
    for g in range(NGR):
        k = g % 2
        b.wpair("tensor", xtg_ready[k])
        own = g < NOG
        if own:
            for gs in range(NGS):
                b.wpair("tensor", b.bank_free.get(6))
                b.wpair("tensor", b.bank_free.get(7))
                for fc in range(NFC):
                    for lc in range(NLC):
                        mm(bank(6 + lc, 0, GSL),
                           wth[:, fc * L + lc * 128: fc * L + lc * 128 + 128],
                           xtg[k][:, fc * GRP + gs * GSL: fc * GRP + (gs + 1) * GSL],
                           start=(fc == 0), stop=(fc == NFC - 1),
                           inc=(fc == NFC - 1 and lc == NLC - 1))
                tmil = b.cnt["sT"]
                b.wait("vector", "sT", tmil)
                th_ev = b.vbatch([
                    (lambda v, c=lc * T + g * GRP + gs * GSL, lc=lc, n=GSL:
                     v.tensor_copy(theta[:, c:c + n], bank(6 + lc, 0, n)))
                    for lc in range(NLC)])
                b.bank_free[6] = ("sV", th_ev)
                b.bank_free[7] = ("sV", th_ev)
        b.wpair("tensor", v_wcon)
        if ev_prev_group is not None:
            b.wait("vector", "sV", ev_prev_group)
        ev = [None] * NGT
        ev_last = None
        for tci in range(NGT):
            bs = (tci % 2) * 3
            for bb in (bs, bs + 1, bs + 2):
                b.wpair("tensor", b.bank_free.get(bb))
            for fc in range(NFC):
                lh = xtg[k][:, fc * GRP + tci * 128: fc * GRP + tci * 128 + 128]
                st, sp = (fc == 0), (fc == NFC - 1)
                mm(bank(bs, 0, L), lh, wph[:, fc * L:(fc + 1) * L], st, sp)
                mm(bank(bs + 1), lh, wg[:, fc * D:fc * D + 512], st, sp)
                mm(bank(bs + 2), lh, wg[:, fc * D + 512:(fc + 1) * D], st, sp,
                   inc=sp)
            tmil = b.cnt["sT"]
            b.wait("vector", "sT", tmil)
            vop(lambda v, tci=tci, bs=bs:
                v.tensor_copy(phg[:, tci * L:(tci + 1) * L], bank(bs, 0, L)))
            vop(lambda v, tci=tci, bs=bs:
                v.tensor_copy(gg[:, tci * D:tci * D + 512], bank(bs + 1)))
            ev_last = vop_inc(lambda v, tci=tci, bs=bs:
                              v.tensor_copy(gg[:, tci * D + 512:(tci + 1) * D],
                                            bank(bs + 2)))
            ev[tci] = ev_last
            for bb in (bs, bs + 1, bs + 2):
                b.bank_free[bb] = ("sV", ev_last)
        ev_prev_group = ev_last
        b.buf_free[("xtg", k)] = ("sT", b.cnt["sT"])
        if g + 2 < NGR:
            load_group(g + 2)
        for lc in range(NLC):
            b.wpair("tensor", b.bank_free.get(6))
            b.wpair("tensor", b.bank_free.get(7))
            for tci in range(NGT):
                b.wait("tensor", "sV", ev[tci])
                for oh in range(2):
                    mm(bank(6 + oh),
                       phg[:, tci * L + lc * 128: tci * L + lc * 128 + 128],
                       gg[:, tci * D + oh * 512: tci * D + oh * 512 + 512],
                       start=(tci == 0), stop=(tci == NGT - 1),
                       inc=(tci == NGT - 1 and oh == 1))
            tmil = b.cnt["sT"]
            b.wait("vector", "sT", tmil)
            b.wait("vector", "sV", mev_prev)
            if g == 0:
                mev_prev = b.vbatch([
                    (lambda v, c=lc * D + oh * 512, bb=6 + oh:
                     v.tensor_copy(macc[:, c:c + 512], bank(bb)))
                    for oh in range(2)])
            else:
                mev_prev = b.vbatch([
                    (lambda v, c=lc * D + oh * 512, bb=6 + oh:
                     v.tensor_add(macc[:, c:c + 512], bank(bb), macc[:, c:c + 512]))
                    for oh in range(2)])
            b.bank_free[6] = ("sV", mev_prev)
            b.bank_free[7] = ("sV", mev_prev)

    if use_cc:
        b.wait("sync", "sV", mev_prev)
        v_ccup = sdma("ccup", ccin_d[:, :, :].rearrange("c p n -> p c n"),
                      reshape3(macc[:, :], NLC))
        b.wpair("gpsimd", v_ccup)
        b.op("gpsimd", lambda g, h=b.sems["ccsm"]: g.collective_compute(
            "AllReduce", mybir.AluOpType.add,
            replica_groups=[[0, 1, 2, 3], [4, 5, 6, 7]],
            ins=[ccin_d.ap().opt()], outs=[ccout_d.ap().opt()]).then_inc(h, 1))
        b.inc("ccsm")
        b.wait("gpsimd", "ccsm", 1)
        b.op("gpsimd", lambda g, h=b.sems["ccdn"]: g.dma_start(
            out=reshape3(macc[:, :], NLC),
            in_=ccout_d[:, :, :].rearrange("c p n -> p c n")).then_inc(h, 16))
        v_ccdn = ("ccdn", b.inc("ccdn", 16))
        b.wpair("scalar", v_ccdn)
        b.wpair("tensor", getattr(b, "v_hcon", None))
        b.wait("tensor", "sV", mev_prev)
        NSPIN = 60
        for k in range(NSPIN):
            mm(bank(7, 0, 128), idn[0:1, 0:128], idn[0:1, 0:128],
               start=(k == 0), stop=(k == NSPIN - 1), inc=(k == NSPIN - 1))
        b.bank_free[7] = ("sT", b.cnt["sT"])

    b.wait("scalar", "sV", mev_prev)
    v_msb = sop_inc(lambda s: s.activation(msb[:, :], macc[:, :], AF.Copy,
                                           bias=0.0, scale=float(SCALE)))

    b.wait("tensor", "sS", v_msb)
    b.wpair("vector", v_x)
    h1_done = None
    for tc in range(NTC):
        bs = (tc % 2) * 2
        b.wpair("tensor", b.bank_free.get(bs))
        b.wpair("tensor", b.bank_free.get(bs + 1))
        for oh in range(2):
            for lc in range(NLC):
                mm(bank(bs + oh),
                   theta[:, lc * T + tc * 128: lc * T + tc * 128 + 128],
                   msb[:, lc * D + oh * 512: lc * D + oh * 512 + 512],
                   start=(lc == 0), stop=(lc == NLC - 1),
                   inc=(lc == NLC - 1 and oh == 1))
        tmil = b.cnt["sT"]
        b.wait("vector", "sT", tmil)
        h1_done = b.vbatch([
            (lambda v, c=tc * D + oh * 512, bb=bs + oh:
             v.tensor_add(resid[:, c:c + 512], bank(bb), resid[:, c:c + 512]))
            for oh in range(2)])
        b.bank_free[bs] = ("sV", h1_done)
        b.bank_free[bs + 1] = ("sV", h1_done)

    def ln_stats(chunks, src, dump_sq, dump_sum, v_pre, t_pre, keep=False,
                 with_ratio=False):
        n = len(chunks)
        b.wait("scalar", "sV", v_pre)
        b.wait("scalar", "sT", t_pre)
        c0s = b.cnt["sS"]
        s_sq = b.chain("scalar", [
            (lambda s, a=src(tc), i=i, dq=dump_sq:
             s.activation(dq, a, AF.Square, accum_out=ssq[:, i:i + 1]))
            for i, tc in enumerate(chunks)])
        b.wait("vector", "sV", v_pre)
        b.wait("vector", "sT", t_pre)
        fns = [
            (lambda v, a=src(tc), i=i, ds=dump_sum:
             v.tensor_scalar(ds, a, 0.0, None, ALU.add, ALU.add,
                             accum_out=sums[:, i:i + 1]))
            for i, tc in enumerate(chunks)]
        fns += [
            lambda v: v.tensor_scalar_mul(mu[:, 0:n], sums[:, 0:n], 1.0 / D),
            lambda v: v.tensor_mul(tmp16[:, 0:n], mu[:, 0:n], mu[:, 0:n]),
        ]
        c0 = b.cnt["sV"]
        p1 = b.chain("vector", fns)
        if keep:
            pts = [("sS", v) for v in range(c0s + 1, s_sq + 1)]
            pts += [("sV", v) for v in range(c0 + 2, p1 + 1, 2)]
            keepalive(pts)
        b.wait("vector", "sV", p1)
        b.wait("vector", "sS", s_sq)
        v_var = b.chain("vector", [
            lambda v: v.scalar_tensor_tensor(tmp16[:, 0:n], ssq[:, 0:n], 1.0 / D,
                                             tmp16[:, 0:n], ALU.mult, ALU.subtract),
            lambda v: v.tensor_scalar_add(tmp16[:, 0:n], tmp16[:, 0:n], float(LN_EPS)),
            lambda v: v.reciprocal(rstd[:, 0:n], tmp16[:, 0:n]),
        ])
        if with_ratio:
            v_var = b.chain("vector", [
                lambda v: v.tensor_scalar(sums[:, 0:n], tmp16[:, 0:n],
                                          1.0 + float(LN_EPS), -float(LN_EPS),
                                          ALU.mult, ALU.add),
                lambda v: v.reciprocal(ssq[:, 0:n], sums[:, 0:n]),
                lambda v: v.tensor_mul(ratio[:, 0:n], tmp16[:, 0:n], ssq[:, 0:n]),
            ])
        b.wait("scalar", "sV", v_var)
        if with_ratio:
            sop(lambda s, n=n: s.activation(ratio[:, 0:n], ratio[:, 0:n], AF.Sqrt))
        s_r = sop_inc(lambda s, n=n: s.activation(rstd[:, 0:n], rstd[:, 0:n], AF.Sqrt))
        b.wait("vector", "sS", s_r)
        if with_ratio:
            b.chain("vector", [
                lambda v, n=n: v.tensor_mul(ratio[:, 0:n], ratio[:, 0:n],
                                            rstd[:, 0:n])])
        return s_r

    def layernorm(chunks, src, dst, dump_sq, dump_sum, v_pre, t_pre, keep=False,
                  with_ratio=False):
        s_r = ln_stats(chunks, src, dump_sq, dump_sum, v_pre, t_pre, keep=keep,
                       with_ratio=with_ratio)
        b.s_r_last = s_r
        return b.vbatch([
            (lambda v, a=src(tc), d=dst(tc), i=i:
             v.tensor_scalar(d, a, mu[:, i:i + 1], rstd[:, i:i + 1],
                             ALU.subtract, ALU.mult))
            for i, tc in enumerate(chunks)])

    s_r1 = ln_stats(
        list(range(NTC)),
        src=lambda tc: resid[:, tc * D:(tc + 1) * D],
        dump_sq=gg[:, 0:D], dump_sum=gg[:, D:2 * D],
        v_pre=h1_done, t_pre=None, keep=True, with_ratio=True)
    b.s_r_last = s_r1
    v_zsc = b.chain("vector", [
        lambda v: v.tensor_copy(mu1[:, 0:NTC], mu[:, 0:NTC]),
        lambda v: v.tensor_scalar_mul(r2x[:, 0:NTC], rstd[:, 0:NTC], 2.0),
    ])
    b.v_zsc = v_zsc

    NU = 2 * NE * NJJ
    NW = 2 * NE * 2

    def unit_src(n):
        e = (n // NJJ) % NE
        jj = n % NJJ
        return wgu_d[e, jj, :, :, :, :].rearrange("a c p n -> p a c n")

    def wd_src(m):
        e = (m // 2) % NE
        p = m % 2
        return wd_d[e, p, :, :, :].rearrange("c p n -> p c n")

    gu_land = {}
    wd_land = {}

    def load_gu(n):
        k = n % 3
        b.wpair("gpsimd", b.buf_free.get(("gub", k)))
        gu_land[n] = gdma(
            f"gub{k}", gub[k][:, :].rearrange("p (a c n) -> p a c n", a=2, c=NFC),
            unit_src(n))

    def load_wd(m):
        b.wpair("gpsimd", b.buf_free.get(("wdb",)))
        wd_land[m] = gdma("wdb0", reshape3(wdb[0][:, :], HIDSLOTS), wd_src(m))

    load_gu(0)
    load_gu(1)
    load_gu(2)
    load_wd(0)

    blkA = nc.Block()
    blkA_g = blkA.__enter__()
    b.emit_into(blkA_g)
    blkA.__exit__(None, None, None)

    for c in reversed(actxs):
        c.__exit__(None, None, None)

    bctxs = []
    xnT = sb("xnT", [128, NFC * TH], BF16, bctxs)
    xe = sb("xe", [128, NFC * TH], BF16, bctxs)
    hid = sb("hid", [128, 2 * HIDSLOTS * TH], BF16, bctxs)
    acc = sb("acc", [128, NTH * D], F32, bctxs)
    z2 = sb("z2", [128, D], F32, bctxs)
    srep = sb("srep", [128, TH], BF16, bctxs)
    sg = [sb("sg0", [128, TH], BF16, bctxs)]
    logits = sb("logits", [128, NTH * NE], F32, bctxs)
    probs = sb("probs", [128, NTH * NE], F32, bctxs)
    mk = sb("mk", [128, NTH * NE], F32, bctxs)
    gates = mk
    gatesT = sb("gatesT", [NE, TH], F32, bctxs)
    rst8 = sums
    rst8b = ssq

    unit_ctr = 0
    wd_ctr = 0
    out_dma = [None, None]
    hid_ready = {}
    last_down_mil = None
    prep = {}

    def emit_srep_xe(e, banks, gT_done, xnT_done, keep=False):
        b.wait("tensor", "sV", gT_done)
        b.wpair("tensor", v_hcon)
        for slc in range(NSL):
            b.wpair("tensor", b.bank_free.get(banks[slc]))
            mm(bank(banks[slc], 0, SL), sel[:, e * 128:(e + 1) * 128],
               gatesT[:, slc * SL:(slc + 1) * SL], start=True, stop=True,
               inc=(slc == NSL - 1))
        tmil = b.cnt["sT"]
        b.wait("vector", "sT", tmil)
        b.wpair("vector", b.buf_free.get(("srep",)))
        v_srep = b.vbatch([
            (lambda v, slc=slc, bk=banks[slc]: v.tensor_copy(
                srep[:, slc * SL:(slc + 1) * SL], bank(bk, 0, SL)))
            for slc in range(NSL)])
        for slc in range(NSL):
            b.bank_free[banks[slc]] = ("sV", v_srep)
        b.wpair("vector", b.buf_free.get(("xe",)))
        b.wait("vector", "sS", xnT_done)
        b.wait("vector", "sV", v_srep)
        c0x = b.cnt["sV"]
        v_xe = b.chain("vector", [
            (lambda v, fc=fc: v.tensor_mul(
                xe[:, fc * TH:(fc + 1) * TH], xnT[:, fc * TH:(fc + 1) * TH],
                srep[:, 0:TH]))
            for fc in range(NFC)])
        if keep:
            keepalive([("sV", v) for v in range(c0x + 2, v_xe + 1, 2)])
        b.buf_free[("srep",)] = ("sV", v_xe)
        return v_xe

    def make_h1_prep_pieces():
        assert NTH == 8
        st = {}
        hc1 = NTH

        def merge(*fs):
            def f():
                for g in fs:
                    g()
            return f

        def z_piece(k):
            def f():
                tc = hc1 + k
                b.wpair("vector", b.buf_free.get(("z", 0)))
                st[("z", k)] = vop_inc(lambda v, tc=tc: v.tensor_scalar(
                    z2[:, 0:D], resid[:, tc * D:(tc + 1) * D],
                    mu1[:, tc:tc + 1], ratio[:, tc:tc + 1],
                    ALU.subtract, ALU.mult))
            return f

        def t_piece(k):
            def f():
                b.wait("tensor", "sV", st[("z", k)])
                b.wpair("tensor", b.bank_free.get(2))
                b.wpair("tensor", b.bank_free.get(3))
                tmil = None
                for fc in range(NFC):
                    bb = 2 + fc // 4
                    off = (fc % 4) * 128
                    if fc < NFC - 1:
                        b.op("tensor", lambda t, fc=fc, bb=bb, off=off:
                             t.transpose(bank(bb, off, 128),
                                         z2[:, fc * 128:(fc + 1) * 128],
                                         idn[:, :]))
                    else:
                        tmil = b.op_inc("tensor", lambda t, fc=fc, bb=bb, off=off:
                                        t.transpose(bank(bb, off, 128),
                                                    z2[:, fc * 128:(fc + 1) * 128],
                                                    idn[:, :]))
                b.buf_free[("z", 0)] = ("sT", tmil)
                b.wait("scalar", "sT", tmil)
                if k == 0:
                    b.wait("scalar", "sV", v_xe)
                ev = None
                for fc in range(NFC):
                    bb = 2 + fc // 4
                    off = (fc % 4) * 128
                    if fc < NFC - 1:
                        sop(lambda s, fc=fc, bb=bb, off=off: s.activation(
                            xnT[:, fc * TH + k * 128: fc * TH + (k + 1) * 128],
                            bank(bb, off, 128), AF.Identity,
                            bias=bet[:, fc:fc + 1], scale=gam[:, fc:fc + 1]))
                    else:
                        ev = b.op_inc("scalar", lambda s, fc=fc, bb=bb, off=off:
                                      s.activation(
                            xnT[:, fc * TH + k * 128: fc * TH + (k + 1) * 128],
                            bank(bb, off, 128), AF.Identity,
                            bias=bet[:, fc:fc + 1], scale=gam[:, fc:fc + 1]))
                b.bank_free[2] = ("sS", ev)
                b.bank_free[3] = ("sS", ev)
                st[("evac", k)] = ev
            return f

        def r_piece(ks, fin):
            def f():
                b.wpair("tensor", v_wcon)
                b.wait("tensor", "sS", st[("evac", max(ks))])
                b.wpair("tensor", b.bank_free.get(2))
                for k in ks:
                    for fc in range(NFC):
                        mm(bank(2, k * NE, NE),
                           xnT[:, fc * TH + k * 128: fc * TH + (k + 1) * 128],
                           rw[:, fc * NE:(fc + 1) * NE],
                           start=(fc == 0), stop=(fc == NFC - 1),
                           inc=(fin and k == ks[-1] and fc == NFC - 1))
                if fin:
                    b.wait("vector", "sT", b.cnt["sT"])
                    st["v_lg"] = vop_inc(lambda v: v.tensor_copy(
                        logits[:, 0:NTH * NE], bank(2, 0, NTH * NE)))
                    b.bank_free[2] = ("sV", st["v_lg"])
            return f

        def smax1():
            lg3 = reshape3(logits[:, 0:NTH * NE], NTH)

            def bc(stat):
                s3 = stat[:, 0:NTH].rearrange("p (c o) -> p c o", o=1)
                _, out = bass.broadcast_tensor_aps(lg3, s3)
                return out

            b.wait("vector", "sV", st["v_lg"])
            v_sub = b.chain("vector", [
                lambda v: v.tensor_scalar(logits[:, 0:NTH * NE],
                                          logits[:, 0:NTH * NE],
                                          float(CLAMP), float(-CLAMP),
                                          ALU.min, ALU.max),
                lambda v: v.tensor_reduce(
                    rst8[:, 0:NTH].rearrange("p (c o) -> p c o", o=1),
                    lg3, AX, ALU.max),
                lambda v: v.tensor_sub(lg3, lg3, bc(rst8)),
            ])
            b.wait("scalar", "sV", v_sub)
            st["s_exp"] = sop_inc(lambda s: s.activation(
                logits[:, 0:NTH * NE], logits[:, 0:NTH * NE], AF.Exp))

        def smax2():
            lg3 = reshape3(logits[:, 0:NTH * NE], NTH)
            pr3 = reshape3(probs[:, 0:NTH * NE], NTH)
            mk3 = reshape3(mk[:, 0:NTH * NE], NTH)

            def bc(stat):
                s3 = stat[:, 0:NTH].rearrange("p (c o) -> p c o", o=1)
                _, out = bass.broadcast_tensor_aps(lg3, s3)
                return out

            b.wait("vector", "sS", st["s_exp"])
            st["v_gt"] = b.chain("vector", [
                lambda v: v.tensor_reduce(
                    rst8[:, 0:NTH].rearrange("p (c o) -> p c o", o=1),
                    lg3, AX, ALU.add),
                lambda v: v.tensor_scalar_add(rst8[:, 0:NTH], rst8[:, 0:NTH],
                                              float(ROUTER_EPS)),
                lambda v: v.reciprocal(rst8b[:, 0:NTH], rst8[:, 0:NTH]),
                lambda v: v.tensor_mul(pr3, lg3, bc(rst8b)),
                lambda v: v.tensor_scalar(probs[:, 0:NTH * NE],
                                          probs[:, 0:NTH * NE],
                                          float(ROUTER_EPS), 1.0,
                                          ALU.max, ALU.min),
                lambda v: v.tensor_reduce(
                    rst8[:, 0:NTH].rearrange("p (c o) -> p c o", o=1),
                    pr3, AX, ALU.max),
                lambda v: v.tensor_tensor(mk3, pr3, bc(rst8), ALU.is_ge),
                lambda v: v.scalar_tensor_tensor(mk[:, 0:NTH * NE],
                                                 mk[:, 0:NTH * NE],
                                                 -1e30, probs[:, 0:NTH * NE],
                                                 ALU.mult, ALU.add),
                lambda v: v.tensor_reduce(
                    rst8b[:, 0:NTH].rearrange("p (c o) -> p c o", o=1),
                    mk3, AX, ALU.max),
                lambda v: v.tensor_tensor(mk3, pr3, bc(rst8b), ALU.is_ge),
                lambda v: v.tensor_mul(gates[:, 0:NTH * NE],
                                       probs[:, 0:NTH * NE],
                                       mk[:, 0:NTH * NE]),
            ])

        def gts():
            b.wait("tensor", "sV", st["v_gt"])
            b.wpair("tensor", b.bank_free.get(2))
            b.wpair("tensor", b.bank_free.get(3))
            tmil = None
            for i in range(NTH):
                bb = 2 + i // 4
                off = (i % 4) * 128
                if i < NTH - 1:
                    b.op("tensor", lambda t, i=i, bb=bb, off=off:
                         t.transpose(PS[0:NE, bb * 512 + off:bb * 512 + off + 128],
                                     gates[:, i * NE:(i + 1) * NE], idn[:, :]))
                else:
                    tmil = b.op_inc(
                        "tensor", lambda t, i=i, bb=bb, off=off:
                        t.transpose(PS[0:NE, bb * 512 + off:bb * 512 + off + 128],
                                    gates[:, i * NE:(i + 1) * NE], idn[:, :]))
            b.wait("vector", "sT", tmil)
            gT_done = b.vbatch([
                (lambda v, bb=bb, w=min(512, NTH * 128 - (bb - 2) * 512):
                 v.tensor_copy(gatesT[:, (bb - 2) * 512:(bb - 2) * 512 + w],
                               PS[0:NE, bb * 512:bb * 512 + w]))
                for bb in range(2, 2 + (NTH * 128 + 511) // 512)])
            b.bank_free[2] = ("sV", gT_done)
            b.bank_free[3] = ("sV", gT_done)
            prep["gT_done"] = gT_done

        def srx():
            prep["xnT_done"] = st[("evac", NTH - 1)]
            b.buf_free[("xe",)] = ("sT", prep["gu15"])
            prep["v_xe"] = emit_srep_xe(0, (2, 3), prep["gT_done"],
                                        prep["xnT_done"])

        P = [lambda: None for _ in range(3 * NTH)]
        P[0] = z_piece(0)
        for k in range(NTH):
            P[1 + 2 * k] = (merge(t_piece(k), z_piece(k + 1))
                            if k + 1 < NTH else t_piece(k))
        P[17] = r_piece((0, 1, 2, 3), fin=False)
        P[18] = r_piece((4, 5, 6, 7), fin=True)
        P[19] = smax1
        P[20] = smax2
        P[21] = gts
        P[22] = srx
        return P

    for h in range(2):
        hc0 = h * NTH
        b.wpair("scalar", v_hcon)
        b.wpair("vector", v_hcon)
        b.wait("vector", "sV", b.v_zsc)
        b.wait("vector", "sS", b.s_r_last)

        if h == 1 and NTH == 8:
            xnT_done = prep["xnT_done"]
            gT_done = prep["gT_done"]
            v_xe = prep["v_xe"]
        else:
          xnT_done = None
          for i in range(NTH):
            tc = hc0 + i
            zs = 0
            b.wpair("vector", b.buf_free.get(("z", zs)))
            v_z = vop_inc(lambda v, tc=tc, zs=zs:
                          v.tensor_scalar(z2[:, zs * D:(zs + 1) * D],
                                          resid[:, tc * D:(tc + 1) * D],
                                          mu1[:, tc:tc + 1], ratio[:, tc:tc + 1],
                                          ALU.subtract, ALU.mult))
            keepalive([("sV", v_z)])
            b.wpair("tensor", v_hcon)
            bs = (i % 2) * 2
            b.wpair("tensor", b.bank_free.get(bs))
            b.wpair("tensor", b.bank_free.get(bs + 1))
            for fc in range(NFC):
                bb = bs + fc // 4
                off = (fc % 4) * 128
                if fc < NFC - 1:
                    b.op("tensor", lambda t, zs=zs, fc=fc, bb=bb, off=off:
                         t.transpose(bank(bb, off, 128),
                                     z2[:, zs * D + fc * 128: zs * D + (fc + 1) * 128],
                                     idn[:, :]))
                else:
                    tmil = b.op_inc("tensor", lambda t, zs=zs, fc=fc, bb=bb, off=off:
                                    t.transpose(bank(bb, off, 128),
                                                z2[:, zs * D + fc * 128: zs * D + (fc + 1) * 128],
                                                idn[:, :]))
            b.buf_free[("z", zs)] = ("sT", tmil)
            b.wait("scalar", "sT", tmil)
            b.wpair("scalar", v_hcon)
            for fc in range(NFC):
                bb = bs + fc // 4
                off = (fc % 4) * 128
                if fc < NFC - 1:
                    sop(lambda s, fc=fc, bb=bb, off=off, i=i: s.activation(
                        xnT[:, fc * TH + i * 128: fc * TH + (i + 1) * 128],
                        bank(bb, off, 128), AF.Identity,
                        bias=bet[:, fc:fc + 1], scale=gam[:, fc:fc + 1]))
                else:
                    xnT_done = b.op_inc("scalar", lambda s, fc=fc, bb=bb, off=off, i=i:
                                        s.activation(
                        xnT[:, fc * TH + i * 128: fc * TH + (i + 1) * 128],
                        bank(bb, off, 128), AF.Identity,
                        bias=bet[:, fc:fc + 1], scale=gam[:, fc:fc + 1]))
            b.bank_free[bs] = ("sS", xnT_done)
            b.bank_free[bs + 1] = ("sS", xnT_done)
            b.wait("tensor", "sS", xnT_done)
            b.wpair("tensor", v_wcon)
            b.wpair("tensor", b.bank_free.get(4))
            for fc in range(NFC):
                mm(bank(4, 0, NE), xnT[:, fc * TH + i * 128: fc * TH + (i + 1) * 128],
                   rw[:, fc * NE:(fc + 1) * NE],
                   start=(fc == 0), stop=(fc == NFC - 1), inc=(fc == NFC - 1))
            tmil = b.cnt["sT"]
            b.wait("vector", "sT", tmil)
            v_lg = vop_inc(lambda v, i=i: v.tensor_copy(
                logits[:, i * NE:(i + 1) * NE], bank(4, 0, NE)))
            b.bank_free[4] = ("sV", v_lg)


          lg3 = reshape3(logits[:, 0:NTH * NE], NTH)
          pr3 = reshape3(probs[:, 0:NTH * NE], NTH)
          mk3 = reshape3(mk[:, 0:NTH * NE], NTH)

          def bc(stat):
            s3 = stat[:, 0:NTH].rearrange("p (c o) -> p c o", o=1)
            _, out = bass.broadcast_tensor_aps(lg3, s3)
            return out

          b.wait("vector", "sV", v_lg)
          c0r = b.cnt["sV"]
          v_sub = b.chain("vector", [
            lambda v: v.tensor_scalar(logits[:, 0:NTH * NE], logits[:, 0:NTH * NE],
                                      float(CLAMP), float(-CLAMP), ALU.min, ALU.max),
            lambda v: v.tensor_reduce(
                rst8[:, 0:NTH].rearrange("p (c o) -> p c o", o=1), lg3, AX, ALU.max),
            lambda v: v.tensor_sub(lg3, lg3, bc(rst8)),
          ])
          b.wait("scalar", "sV", v_sub)
          s_exp = sop_inc(lambda s: s.activation(logits[:, 0:NTH * NE],
                                                 logits[:, 0:NTH * NE], AF.Exp))
          b.wait("vector", "sS", s_exp)
          v_gt = b.chain("vector", [
            lambda v: v.tensor_reduce(
                rst8[:, 0:NTH].rearrange("p (c o) -> p c o", o=1), lg3, AX, ALU.add),
            lambda v: v.tensor_scalar_add(rst8[:, 0:NTH], rst8[:, 0:NTH],
                                          float(ROUTER_EPS)),
            lambda v: v.reciprocal(rst8b[:, 0:NTH], rst8[:, 0:NTH]),
            lambda v: v.tensor_mul(pr3, lg3, bc(rst8b)),
            lambda v: v.tensor_scalar(probs[:, 0:NTH * NE], probs[:, 0:NTH * NE],
                                      float(ROUTER_EPS), 1.0, ALU.max, ALU.min),
            lambda v: v.tensor_reduce(
                rst8[:, 0:NTH].rearrange("p (c o) -> p c o", o=1), pr3, AX, ALU.max),
            lambda v: v.tensor_tensor(mk3, pr3, bc(rst8), ALU.is_ge),
            lambda v: v.scalar_tensor_tensor(mk[:, 0:NTH * NE], mk[:, 0:NTH * NE],
                                             -1e30, probs[:, 0:NTH * NE],
                                             ALU.mult, ALU.add),
            lambda v: v.tensor_reduce(
                rst8b[:, 0:NTH].rearrange("p (c o) -> p c o", o=1), mk3, AX, ALU.max),
            lambda v: v.tensor_tensor(mk3, pr3, bc(rst8b), ALU.is_ge),
            lambda v: v.tensor_mul(gates[:, 0:NTH * NE],
                                   probs[:, 0:NTH * NE], mk[:, 0:NTH * NE]),
          ])
          keepalive([("sV", v) for v in range(c0r + 2, v_gt + 1, 3)])

          b.wait("tensor", "sV", v_gt)
          b.wpair("tensor", b.bank_free.get(4))
          b.wpair("tensor", b.bank_free.get(5))
          tmil = None
          for i in range(NTH):
            bb = 4 + i // 4
            off = (i % 4) * 128
            if i < NTH - 1:
                b.op("tensor", lambda t, i=i, bb=bb, off=off:
                     t.transpose(PS[0:NE, bb * 512 + off:bb * 512 + off + 128],
                                 gates[:, i * NE:(i + 1) * NE], idn[:, :]))
            else:
                tmil = b.op_inc("tensor", lambda t, i=i, bb=bb, off=off:
                                t.transpose(PS[0:NE, bb * 512 + off:bb * 512 + off + 128],
                                            gates[:, i * NE:(i + 1) * NE], idn[:, :]))
          b.wait("vector", "sT", tmil)
          b.wait("vector", "sV", v_gt)
          gT_done = b.vbatch([
            (lambda v, bb=bb, w=min(512, NTH * 128 - (bb - 4) * 512):
             v.tensor_copy(gatesT[:, (bb - 4) * 512:(bb - 4) * 512 + w],
                           PS[0:NE, bb * 512:bb * 512 + w]))
            for bb in range(4, 4 + (NTH * 128 + 511) // 512)])
          b.bank_free[4] = ("sV", gT_done)
          b.bank_free[5] = ("sV", gT_done)

          v_xe = emit_srep_xe(0, (2, 3), gT_done, xnT_done, keep=True)


        def gu_pass(p, e):
            nonlocal unit_ctr
            hset = (p % 2) * HIDSLOTS
            for j2 in range(HIDSLOTS):
                uk = unit_ctr % 3
                b.wpair("tensor", gu_land[unit_ctr])
                b.wpair("tensor", b.buf_free.get(("hid", p % 2, j2)))
                gb = 4 if j2 % 2 == 0 else 0
                ub = 6 if j2 % 2 == 0 else 2
                mils = {}
                for a, bs in ((0, gb), (1, ub)):
                    for slc in range(NSL):
                        b.wpair("tensor", b.bank_free.get(bs + slc))
                    for fc in range(NFC):
                        for slc in range(NSL):
                            mm(bank(bs + slc, 0, SL),
                               gub[uk][:, a * D + fc * 128:a * D + (fc + 1) * 128],
                               xe[:, fc * TH + slc * SL: fc * TH + (slc + 1) * SL],
                               start=(fc == 0), stop=(fc == NFC - 1),
                               inc=(fc == NFC - 1 and slc == NSL - 1))
                    mils[a] = b.cnt["sT"]
                b.buf_free[("gub", uk)] = ("sT", mils[1])
                if unit_ctr + 3 < NU:
                    load_gu(unit_ctr + 3)
                sgk = sg[0]
                b.wait("scalar", "sT", mils[0])
                b.wpair("scalar", b.buf_free.get(("sg",)))
                for slc in range(NSL):
                    if slc < NSL - 1:
                        sop(lambda s, slc=slc, sgk=sgk, gb=gb: s.activation(
                            sgk[:, slc * SL:(slc + 1) * SL], bank(gb + slc, 0, SL),
                            AF.Silu))
                    else:
                        s_silu = b.op_inc("scalar", lambda s, slc=slc, sgk=sgk, gb=gb:
                                          s.activation(sgk[:, slc * SL:(slc + 1) * SL],
                                                       bank(gb + slc, 0, SL), AF.Silu))
                for slc in range(NSL):
                    b.bank_free[gb + slc] = ("sS", s_silu)
                b.wait("vector", "sT", mils[1])
                b.wait("vector", "sS", s_silu)
                v_hid = vop_inc(lambda v, c=(hset + j2) * TH, sgk=sgk, ub=ub:
                                v.tensor_mul(hid[:, c:c + TH], sgk[:, 0:TH],
                                             PS[:, ub * 512: ub * 512 + TH]))
                hid_ready[(p % 2, j2)] = v_hid
                b.buf_free[("sg",)] = ("sV", v_hid)
                for slc in range(NSL):
                    b.bank_free[ub + slc] = ("sV", v_hid)
                unit_ctr += 1
            return b.cnt["sT"]

        def down_pass(p, first, srep_fn=None, weave=None):
            nonlocal wd_ctr, acc_ev, last_down_mil
            hset = (p % 2) * HIDSLOTS
            b.wpair("tensor", wd_land[wd_ctr])
            b.wpair("vector", b.buf_free.get(("accw",)))
            pairs = (4, 6, 0)
            srep_at = min(2, NTH - 1)
            for i in range(NTH):
                if i == srep_at and srep_fn is not None:
                    srep_fn()
                if weave:
                    weave.pop(0)()
                bs = pairs[i % 3]
                b.wpair("tensor", b.bank_free.get(bs))
                b.wpair("tensor", b.bank_free.get(bs + 1))
                for ffc in range(HIDSLOTS):
                    b.wait("tensor", "sV", hid_ready[(p % 2, ffc)])
                    for oh in range(2):
                        mm(bank(bs + oh),
                           hid[:, (hset + ffc) * TH + i * 128:
                               (hset + ffc) * TH + (i + 1) * 128],
                           wdb[0][:, ffc * D + oh * 512: ffc * D + oh * 512 + 512],
                           start=(ffc == 0), stop=(ffc == HIDSLOTS - 1),
                           inc=(ffc == HIDSLOTS - 1 and oh == 1))
                tmil = b.cnt["sT"]
                b.wait("vector", "sT", tmil)
                if h == 1 and first and i == 0:
                    b.wpair("vector", out_dma[0])
                if first:
                    acc_ev = b.vbatch([
                        (lambda v, c=i * D + oh * 512, bb=bs + oh:
                         v.tensor_copy(acc[:, c:c + 512], bank(bb)))
                        for oh in range(2)])
                else:
                    acc_ev = b.vbatch([
                        (lambda v, c=i * D + oh * 512, bb=bs + oh:
                         v.tensor_add(acc[:, c:c + 512], bank(bb),
                                      acc[:, c:c + 512]))
                        for oh in range(2)])
                b.bank_free[bs] = ("sV", acc_ev)
                b.bank_free[bs + 1] = ("sV", acc_ev)
            b.buf_free[("accw",)] = ("sV", acc_ev)
            last_down_mil = b.cnt["sT"]
            for j2 in range(HIDSLOTS):
                b.buf_free[("hid", p % 2, j2)] = ("sT", last_down_mil)
            b.buf_free[("wdb",)] = ("sT", last_down_mil)
            if wd_ctr + 1 < NW:
                load_wd(wd_ctr + 1)
            wd_ctr += 1

        wv = make_h1_prep_pieces() if (h == 0 and NTH == 8) else None
        acc_ev = None
        pending = None
        for p in range(2 * NE):
            e, hp = p // 2, p % 2
            if hp == 0:
                b.wait("tensor", "sV", v_xe)
            gu_last = gu_pass(p, e)
            if h == 0 and p == 2 * NE - 1:
                prep["gu15"] = gu_last
            srep_fn = None
            if hp == 1 and e + 1 < NE:
                b.buf_free[("xe",)] = ("sT", gu_last)

                def srep_fn(e=e):
                    nonlocal v_xe
                    v_xe = emit_srep_xe(e + 1, (2, 3), gT_done, xnT_done)
            if pending is not None:
                down_pass(pending, first=(pending == 0), srep_fn=srep_fn,
                          weave=(wv if wv is not None and pending >= 13
                                 else None))
            pending = p
        down_pass(pending, first=False, srep_fn=None, weave=wv)
        if wv is not None:
            assert not wv, f"unconsumed weave pieces: {len(wv)}"

        b.wait("vector", "sV", acc_ev)
        fns_h2 = []
        for i in range(NTH):
            tc = hc0 + i
            fns_h2.append(lambda v, tc=tc: v.tensor_scalar(
                resid[:, tc * D:(tc + 1) * D], resid[:, tc * D:(tc + 1) * D],
                mu1[:, tc:tc + 1], r2x[:, tc:tc + 1], ALU.subtract, ALU.mult))
            fns_h2.append(lambda v, tc=tc, i=i: v.tensor_add(
                resid[:, tc * D:(tc + 1) * D], resid[:, tc * D:(tc + 1) * D],
                acc[:, i * D:(i + 1) * D]))
        h2_done = b.chain("vector", fns_h2)
        ln3_done = layernorm(
            list(range(hc0, hc0 + NTH)),
            src=lambda tc: resid[:, tc * D:(tc + 1) * D],
            dst=lambda tc: acc[:, (tc - hc0) * D:(tc - hc0 + 1) * D],
            dump_sq=hid[:, 0:D], dump_sum=z2[:, 0:D],
            v_pre=h2_done, t_pre=last_down_mil,
            keep=(h == 0 and NTH != 8))
        b.wait("sync", "sV", ln3_done)
        out_dma[h] = sdma(f"out{h}", y_d[hc0:hc0 + NTH, :, :].rearrange("c p n -> p c n"),
                          reshape3(acc[:, 0:NTH * D], NTH))

    b.wpair("sync", out_dma[0])
    b.wpair("sync", out_dma[1])

    blkB = nc.Block()
    blkB_g = blkB.__enter__()
    b.emit_into(blkB_g)
    blkB.__exit__(None, None, None)

    for c in reversed(bctxs):
        c.__exit__(None, None, None)
    for c in reversed(ctxs):
        c.__exit__(None, None, None)
    return nc


def kernel(L0_attn_x, L0_attn_W_theta, L0_attn_W_phi, L0_attn_W_g,
           L0_ffn_ln_gamma, L0_ffn_ln_beta, L0_ffn_router_w,
           L0_ffn_gate_up_w, L0_ffn_down_w):
    x = np.ascontiguousarray(np.asarray(L0_attn_x, dtype=np.float32))
    B, S, _ = x.shape
    NCORES = 8
    CPB = NCORES // B
    T = S // CPB

    def wT_chunks(w):
        return np.ascontiguousarray(np.asarray(w, np.float32).T.reshape(NFC, 128, -1))

    wth_h = wT_chunks(L0_attn_W_theta)
    wph_h = wT_chunks(L0_attn_W_phi)
    wg_h = wT_chunks(L0_attn_W_g)
    rw_h = wT_chunks(L0_ffn_router_w)
    gam_h = np.ascontiguousarray(np.asarray(L0_ffn_ln_gamma, np.float32).reshape(NFC, 128).T)
    bet_h = np.ascontiguousarray(np.asarray(L0_ffn_ln_beta, np.float32).reshape(NFC, 128).T)
    wgu = np.asarray(L0_ffn_gate_up_w, np.float32)
    wgu_h = np.ascontiguousarray(
        wgu.reshape(NE, 2, NJJ, 128, D).transpose(0, 2, 1, 4, 3)
           .reshape(NE, NJJ, 2, NFC, 128, 128))
    wd = np.asarray(L0_ffn_down_w, np.float32)
    wd_h = np.ascontiguousarray(
        wd.transpose(0, 2, 1).reshape(NE, 2, HIDSLOTS, 128, D))
    sel_h = np.zeros((NE, NE * 128), np.float32)
    for e in range(NE):
        sel_h[e, e * 128:(e + 1) * 128] = 1.0
    idn_h = np.eye(128, dtype=np.float32)

    xT_batches = [np.ascontiguousarray(x[bb].T) for bb in range(B)]

    nc = build_program(T=T, TB=S, GRP=512)
    in_maps = []
    for c in range(NCORES):
        bb = c // CPB
        s0 = (c % CPB) * T
        xt = np.ascontiguousarray(x[bb, s0:s0 + T].reshape(T // 128, 128, D))
        xTb = xT_batches[bb]
        if USE_CC:
            xT_core = np.ascontiguousarray(xTb[:, s0:s0 + T].reshape(NFC, 128, T))
        else:
            xT_core = np.ascontiguousarray(
                np.concatenate([xTb[:, s0:s0 + T], xTb[:, :s0], xTb[:, s0 + T:]],
                               axis=1).reshape(NFC, 128, S))
        in_maps.append({
            "xtok": xt, "xT": xT_core,
            "wth": wth_h, "wph": wph_h, "wg": wg_h, "rw": rw_h,
            "gam": gam_h, "bet": bet_h,
            "wgu": wgu_h, "wd": wd_h, "sel": sel_h, "idn": idn_h,
        })
    res = run_bass_kernel_spmd(nc, in_maps, core_ids=list(range(NCORES)))
    outs = [np.asarray(res.results[c]["y"]).reshape(T, D) for c in range(NCORES)]
    full = np.concatenate(outs, axis=0).reshape(B, S, D)
    return full.astype(np.float32)


# version 29
# speedup vs baseline: 1.0348x; 1.0014x over previous
import numpy as np
import concourse.bass as bass
import concourse.mybir as mybir
from concourse.bass_utils import run_bass_kernel_spmd

F32 = mybir.dt.float32
BF16 = mybir.dt.bfloat16
AF = mybir.ActivationFunctionType
ALU = mybir.AluOpType
AX = mybir.AxisListType.X

D = 1024
NFC = D // 128
L = 256
NLC = L // 128
NE = 8
NJJ = 16
HIDSLOTS = 8
USE_CC = False
SCALE = 1.0 / 32.0
LN_EPS = 1e-5
ROUTER_EPS = 1e-8
CLAMP = 10000.0


class Builder:
    ENGS = ("sync", "gpsimd", "tensor", "vector", "scalar")
    OWN = {"tensor": "sT", "vector": "sV", "scalar": "sS"}

    def __init__(self, nc):
        self.nc = nc
        self.ops = {e: [] for e in self.ENGS}
        self.sems = {}
        self.cnt = {}
        self.last_wait = {}
        self.bank_free = {}
        self.buf_free = {}

    def sem(self, name, ctxs):
        h = self.nc.semaphore(name)
        ctxs.append(h)
        self.sems[name] = h.__enter__()
        self.cnt[name] = 0

    def op(self, eng, fn):
        self.ops[eng].append(fn)

    def wait(self, eng, sem, val):
        if val is None or val <= 0:
            return
        assert val <= self.cnt[sem], f"wait {eng} {sem}>={val} not yet emitted"
        key = (eng, sem)
        if self.last_wait.get(key, -1) >= val:
            return
        self.last_wait[key] = val
        h = self.sems[sem]
        self.op(eng, lambda e, h=h, v=val: e.wait_ge(h, v))

    def wpair(self, eng, pair):
        if pair is not None:
            self.wait(eng, pair[0], pair[1])

    def inc(self, sem, n=1):
        self.cnt[sem] += n
        return self.cnt[sem]

    def chain(self, eng, fns):
        sem = self.OWN[eng]
        h = self.sems[sem]
        last = None
        for f in fns:
            if last is not None:
                self.wait(eng, sem, last)
            self.op(eng, lambda e, f=f, h=h: f(e).then_inc(h, 1))
            last = self.inc(sem)
        return last

    def vbatch(self, fns):
        h = self.sems["sV"]
        for f in fns[:-1]:
            self.op("vector", f)
        self.op("vector", lambda v, f=fns[-1], h=h: f(v).then_inc(h, 1))
        return self.inc("sV")

    def op_inc(self, eng, fn):
        sem = self.OWN[eng]
        h = self.sems[sem]
        self.op(eng, lambda e, f=fn, h=h: f(e).then_inc(h, 1))
        return self.inc(sem)

    def emit_into(self, block):
        for eng in self.ENGS:
            if not self.ops[eng]:
                continue
            fns = self.ops[eng]
            getattr(block, eng)(lambda e, fns=fns: [f(e) for f in fns])
        self.ops = {e: [] for e in self.ENGS}


def build_program(T=2048, TB=8192, GRP=512, use_cc=USE_CC):
    if use_cc:
        TB = T
    NTC = T // 128
    TH = T // 2
    NTH = TH // 128
    SL = min(512, TH)
    NSL = TH // SL
    NGT = GRP // 128
    NGR = TB // GRP
    NOG = T // GRP
    GSL = min(512, GRP)
    NGS = GRP // GSL

    nc = bass.Bass("TRN2", target_bir_lowering=False)
    b = Builder(nc)
    ctxs = []

    def sb(name, shape, dt, clist):
        h = nc.sbuf_tensor(name, shape, dt)
        clist.append(h)
        return h.__enter__()

    xtok = nc.dram_tensor("xtok", [NTC, 128, D], F32, kind="ExternalInput")
    xT_d = nc.dram_tensor("xT", [NFC, 128, TB], F32, kind="ExternalInput")
    wth_d = nc.dram_tensor("wth", [NFC, 128, L], F32, kind="ExternalInput")
    wph_d = nc.dram_tensor("wph", [NFC, 128, L], F32, kind="ExternalInput")
    wg_d = nc.dram_tensor("wg", [NFC, 128, D], F32, kind="ExternalInput")
    rw_d = nc.dram_tensor("rw", [NFC, 128, NE], F32, kind="ExternalInput")
    gam_d = nc.dram_tensor("gam", [128, NFC], F32, kind="ExternalInput")
    bet_d = nc.dram_tensor("bet", [128, NFC], F32, kind="ExternalInput")
    wgu_d = nc.dram_tensor("wgu", [NE, NJJ, 2, NFC, 128, 128], F32, kind="ExternalInput")
    wd_d = nc.dram_tensor("wd", [NE, 2, HIDSLOTS, 128, D], F32, kind="ExternalInput")
    sel_d = nc.dram_tensor("sel", [NE, NE * 128], F32, kind="ExternalInput")
    idn_d = nc.dram_tensor("idn", [128, 128], F32, kind="ExternalInput")
    y_d = nc.dram_tensor("y", [NTC, 128, D], F32, kind="ExternalOutput")
    if use_cc:
        ccin_d = nc.dram_tensor("ccin", [NLC, 128, D], F32)
        ccout_d = nc.dram_tensor("ccout", [NLC, 128, D], F32)

    resid = sb("resid", [128, NTC * D], F32, ctxs)
    gub = [sb(f"gub{k}", [128, 2 * D], BF16, ctxs) for k in range(3)]
    wdb = [sb("wdb0", [128, HIDSLOTS * D], BF16, ctxs)]
    rw = sb("s_rw", [128, NE * NFC], BF16, ctxs)
    gam = sb("s_gam", [128, NFC], F32, ctxs)
    bet = sb("s_bet", [128, NFC], F32, ctxs)
    idn = sb("s_idn", [128, 128], F32, ctxs)
    sel = sb("s_sel", [NE, NE * 128], F32, ctxs)
    sums = sb("sums", [128, NTC], F32, ctxs)
    ssq = sb("ssq", [128, NTC], F32, ctxs)
    mu = sb("mu", [128, NTC], F32, ctxs)
    tmp16 = sb("tmp16", [128, NTC], F32, ctxs)
    rstd = sb("rstd", [128, NTC], F32, ctxs)
    ratio = sb("ratio", [128, NTC], F32, ctxs)
    mu1 = sb("mu1", [128, NTC], F32, ctxs)
    r2x = sb("r2x", [128, NTC], F32, ctxs)

    for s in ("sT", "sV", "sS", "wcon", "hcon", "hx", "out0", "out1",
              "xtg0", "xtg1", "gub0", "gub1", "gub2", "wdb0", "wth1",
              "ccup", "ccsm", "ccdn"):
        b.sem(s, ctxs)

    PSh = nc.psum_tensor("PS", [128, 4096], F32)
    ctxs.append(PSh)
    PS = PSh.__enter__()

    def bank(i, off=0, n=512):
        return PS[:, i * 512 + off:i * 512 + off + n]

    def gdma(sem, dst_ap, src_ap):
        b.op("gpsimd", lambda e, d=dst_ap, s=src_ap, h=b.sems[sem]:
             e.dma_start(d, s).then_inc(h, 16))
        return (sem, b.inc(sem, 16))

    def sdma(sem, dst_ap, src_ap, slow=False):
        b.op("sync", lambda e, d=dst_ap, s=src_ap, h=b.sems[sem], sl=slow:
             e.dma_start(d, s, allow_slow_non_contiguous=sl).then_inc(h, 16))
        return (sem, b.inc(sem, 16))

    def mm(out_ap, lh_ap, rh_ap, start, stop, inc=False):
        if inc:
            b.op("tensor", lambda t, o=out_ap, l=lh_ap, r=rh_ap, st=start, sp=stop,
                 h=b.sems["sT"]: t.matmul(o, l, r, start=st, stop=sp).then_inc(h, 1))
            return b.inc("sT")
        b.op("tensor", lambda t, o=out_ap, l=lh_ap, r=rh_ap, st=start, sp=stop:
             t.matmul(o, l, r, start=st, stop=sp))
        return None

    def vop(fn):
        b.op("vector", fn)

    def vop_inc(fn):
        b.op("vector", lambda v, f=fn, h=b.sems["sV"]: f(v).then_inc(h, 1))
        return b.inc("sV")

    def sop(fn):
        b.op("scalar", fn)

    def sop_inc(fn):
        b.op("scalar", lambda s, f=fn, h=b.sems["sS"]: f(s).then_inc(h, 1))
        return b.inc("sS")

    def reshape3(t_ap, c):
        return t_ap.rearrange("p (c n) -> p c n", c=c)


    def keepalive(points):
        if not points:
            return
        b.wpair("tensor", getattr(b, "v_hcon", None))
        n = len(points)
        for k, (sem, val) in enumerate(points):
            b.wait("tensor", sem, val)
            mm(bank(7, 0, 128), idn[0:1, 0:128], idn[0:1, 0:128],
               start=(k == 0), stop=(k == n - 1), inc=True)
        b.bank_free[7] = ("sT", b.cnt["sT"])

    actxs = []
    xtg = [sb(f"xtg{k}", [128, NFC * GRP], BF16, actxs) for k in range(2)]
    phg = sb("phg", [128, NGT * L], BF16, actxs)
    gg = sb("gg", [128, NGT * D], BF16, actxs)
    theta = sb("theta", [128, NLC * T], BF16, actxs)
    macc = sb("macc", [128, NLC * D], F32, actxs)
    msb = sb("msb", [128, NLC * D], BF16, actxs)
    wth = sb("s_wth", [128, NFC * L], BF16, actxs)
    wph = sb("s_wph", [128, NFC * L], BF16, actxs)
    wg = sb("s_wg", [128, NFC * D], BF16, actxs)

    v_wth = gdma("wth1", reshape3(wth[:, :], NFC),
                 wth_d[:, :, :].rearrange("c p n -> p c n"))
    sdma("hcon", gam[:, :], gam_d[:, :])
    sdma("hcon", bet[:, :], bet_d[:, :])
    sdma("hcon", idn[:, :], idn_d[:, :])
    v_hcon = sdma("hcon", sel[:, :], sel_d[:, :])
    b.v_hcon = v_hcon
    v_x = sdma("hx", reshape3(resid[:, :], NTC),
               xtok[:, :, :].rearrange("c p n -> p c n"))

    xtg_ready = [None, None]

    def load_group(g):
        k = g % 2
        b.wpair("gpsimd", b.buf_free.get(("xtg", k)))
        xtg_ready[k] = gdma(
            f"xtg{k}", reshape3(xtg[k][:, :], NFC),
            xT_d[:, :, g * GRP:(g + 1) * GRP].rearrange("c p n -> p c n"))

    load_group(0)
    gdma("wcon", reshape3(wph[:, :], NFC), wph_d[:, :, :].rearrange("c p n -> p c n"))
    gdma("wcon", reshape3(wg[:, :], NFC), wg_d[:, :, :].rearrange("c p n -> p c n"))
    v_wcon = gdma("wcon", reshape3(rw[:, :], NFC),
                  rw_d[:, :, :].rearrange("c p n -> p c n"))
    if NGR > 1:
        load_group(1)

    b.wpair("tensor", v_wth)
    mev_prev = None
    ev_prev_group = None
    for g in range(NGR):
        k = g % 2
        b.wpair("tensor", xtg_ready[k])
        own = g < NOG
        if own:
            for gs in range(NGS):
                b.wpair("tensor", b.bank_free.get(6))
                b.wpair("tensor", b.bank_free.get(7))
                for fc in range(NFC):
                    for lc in range(NLC):
                        mm(bank(6 + lc, 0, GSL),
                           wth[:, fc * L + lc * 128: fc * L + lc * 128 + 128],
                           xtg[k][:, fc * GRP + gs * GSL: fc * GRP + (gs + 1) * GSL],
                           start=(fc == 0), stop=(fc == NFC - 1),
                           inc=(fc == NFC - 1 and lc == NLC - 1))
                tmil = b.cnt["sT"]
                b.wait("vector", "sT", tmil)
                th_ev = b.vbatch([
                    (lambda v, c=lc * T + g * GRP + gs * GSL, lc=lc, n=GSL:
                     v.tensor_copy(theta[:, c:c + n], bank(6 + lc, 0, n)))
                    for lc in range(NLC)])
                b.bank_free[6] = ("sV", th_ev)
                b.bank_free[7] = ("sV", th_ev)
        b.wpair("tensor", v_wcon)
        if ev_prev_group is not None:
            b.wait("vector", "sV", ev_prev_group)
        ev = [None] * NGT
        ev_last = None
        for tci in range(NGT):
            bs = (tci % 2) * 3
            for bb in (bs, bs + 1, bs + 2):
                b.wpair("tensor", b.bank_free.get(bb))
            for fc in range(NFC):
                lh = xtg[k][:, fc * GRP + tci * 128: fc * GRP + tci * 128 + 128]
                st, sp = (fc == 0), (fc == NFC - 1)
                mm(bank(bs, 0, L), lh, wph[:, fc * L:(fc + 1) * L], st, sp)
                mm(bank(bs + 1), lh, wg[:, fc * D:fc * D + 512], st, sp)
                mm(bank(bs + 2), lh, wg[:, fc * D + 512:(fc + 1) * D], st, sp,
                   inc=sp)
            tmil = b.cnt["sT"]
            b.wait("vector", "sT", tmil)
            vop(lambda v, tci=tci, bs=bs:
                v.tensor_copy(phg[:, tci * L:(tci + 1) * L], bank(bs, 0, L)))
            vop(lambda v, tci=tci, bs=bs:
                v.tensor_copy(gg[:, tci * D:tci * D + 512], bank(bs + 1)))
            ev_last = vop_inc(lambda v, tci=tci, bs=bs:
                              v.tensor_copy(gg[:, tci * D + 512:(tci + 1) * D],
                                            bank(bs + 2)))
            ev[tci] = ev_last
            for bb in (bs, bs + 1, bs + 2):
                b.bank_free[bb] = ("sV", ev_last)
        ev_prev_group = ev_last
        b.buf_free[("xtg", k)] = ("sT", b.cnt["sT"])
        if g + 2 < NGR:
            load_group(g + 2)
        for lc in range(NLC):
            b.wpair("tensor", b.bank_free.get(6))
            b.wpair("tensor", b.bank_free.get(7))
            for tci in range(NGT):
                b.wait("tensor", "sV", ev[tci])
                for oh in range(2):
                    mm(bank(6 + oh),
                       phg[:, tci * L + lc * 128: tci * L + lc * 128 + 128],
                       gg[:, tci * D + oh * 512: tci * D + oh * 512 + 512],
                       start=(tci == 0), stop=(tci == NGT - 1),
                       inc=(tci == NGT - 1 and oh == 1))
            tmil = b.cnt["sT"]
            b.wait("vector", "sT", tmil)
            b.wait("vector", "sV", mev_prev)
            if g == 0:
                mev_prev = b.vbatch([
                    (lambda v, c=lc * D + oh * 512, bb=6 + oh:
                     v.tensor_copy(macc[:, c:c + 512], bank(bb)))
                    for oh in range(2)])
            else:
                mev_prev = b.vbatch([
                    (lambda v, c=lc * D + oh * 512, bb=6 + oh:
                     v.tensor_add(macc[:, c:c + 512], bank(bb), macc[:, c:c + 512]))
                    for oh in range(2)])
            b.bank_free[6] = ("sV", mev_prev)
            b.bank_free[7] = ("sV", mev_prev)

    if use_cc:
        b.wait("sync", "sV", mev_prev)
        v_ccup = sdma("ccup", ccin_d[:, :, :].rearrange("c p n -> p c n"),
                      reshape3(macc[:, :], NLC))
        b.wpair("gpsimd", v_ccup)
        b.op("gpsimd", lambda g, h=b.sems["ccsm"]: g.collective_compute(
            "AllReduce", mybir.AluOpType.add,
            replica_groups=[[0, 1, 2, 3], [4, 5, 6, 7]],
            ins=[ccin_d.ap().opt()], outs=[ccout_d.ap().opt()]).then_inc(h, 1))
        b.inc("ccsm")
        b.wait("gpsimd", "ccsm", 1)
        b.op("gpsimd", lambda g, h=b.sems["ccdn"]: g.dma_start(
            out=reshape3(macc[:, :], NLC),
            in_=ccout_d[:, :, :].rearrange("c p n -> p c n")).then_inc(h, 16))
        v_ccdn = ("ccdn", b.inc("ccdn", 16))
        b.wpair("scalar", v_ccdn)
        b.wpair("tensor", getattr(b, "v_hcon", None))
        b.wait("tensor", "sV", mev_prev)
        NSPIN = 60
        for k in range(NSPIN):
            mm(bank(7, 0, 128), idn[0:1, 0:128], idn[0:1, 0:128],
               start=(k == 0), stop=(k == NSPIN - 1), inc=(k == NSPIN - 1))
        b.bank_free[7] = ("sT", b.cnt["sT"])

    b.wait("scalar", "sV", mev_prev)
    v_msb = sop_inc(lambda s: s.activation(msb[:, :], macc[:, :], AF.Copy,
                                           bias=0.0, scale=float(SCALE)))

    b.wait("tensor", "sS", v_msb)
    b.wpair("vector", v_x)
    h1_done = None
    for tc in range(NTC):
        bs = (tc % 2) * 2
        b.wpair("tensor", b.bank_free.get(bs))
        b.wpair("tensor", b.bank_free.get(bs + 1))
        for oh in range(2):
            for lc in range(NLC):
                mm(bank(bs + oh),
                   theta[:, lc * T + tc * 128: lc * T + tc * 128 + 128],
                   msb[:, lc * D + oh * 512: lc * D + oh * 512 + 512],
                   start=(lc == 0), stop=(lc == NLC - 1),
                   inc=(lc == NLC - 1 and oh == 1))
        tmil = b.cnt["sT"]
        b.wait("vector", "sT", tmil)
        h1_done = b.vbatch([
            (lambda v, c=tc * D + oh * 512, bb=bs + oh:
             v.tensor_add(resid[:, c:c + 512], bank(bb), resid[:, c:c + 512]))
            for oh in range(2)])
        b.bank_free[bs] = ("sV", h1_done)
        b.bank_free[bs + 1] = ("sV", h1_done)

    def ln_stats(chunks, src, dump_sq, dump_sum, v_pre, t_pre, keep=False,
                 with_ratio=False):
        n = len(chunks)
        b.wait("scalar", "sV", v_pre)
        b.wait("scalar", "sT", t_pre)
        c0s = b.cnt["sS"]
        s_sq = b.chain("scalar", [
            (lambda s, a=src(tc), i=i, dq=dump_sq:
             s.activation(dq, a, AF.Square, accum_out=ssq[:, i:i + 1]))
            for i, tc in enumerate(chunks)])
        b.wait("vector", "sV", v_pre)
        b.wait("vector", "sT", t_pre)
        fns = [
            (lambda v, a=src(tc), i=i, ds=dump_sum:
             v.tensor_scalar(ds, a, 0.0, None, ALU.add, ALU.add,
                             accum_out=sums[:, i:i + 1]))
            for i, tc in enumerate(chunks)]
        fns += [
            lambda v: v.tensor_scalar_mul(mu[:, 0:n], sums[:, 0:n], 1.0 / D),
            lambda v: v.tensor_mul(tmp16[:, 0:n], mu[:, 0:n], mu[:, 0:n]),
        ]
        c0 = b.cnt["sV"]
        p1 = b.chain("vector", fns)
        if keep:
            pts = [("sS", v) for v in range(c0s + 1, s_sq + 1)]
            pts += [("sV", v) for v in range(c0 + 2, p1 + 1, 2)]
            keepalive(pts)
        b.wait("vector", "sV", p1)
        b.wait("vector", "sS", s_sq)
        v_var = b.chain("vector", [
            lambda v: v.scalar_tensor_tensor(tmp16[:, 0:n], ssq[:, 0:n], 1.0 / D,
                                             tmp16[:, 0:n], ALU.mult, ALU.subtract),
            lambda v: v.tensor_scalar_add(tmp16[:, 0:n], tmp16[:, 0:n], float(LN_EPS)),
            lambda v: v.reciprocal(rstd[:, 0:n], tmp16[:, 0:n]),
        ])
        if with_ratio:
            v_var = b.chain("vector", [
                lambda v: v.tensor_scalar(sums[:, 0:n], tmp16[:, 0:n],
                                          1.0 + float(LN_EPS), -float(LN_EPS),
                                          ALU.mult, ALU.add),
                lambda v: v.reciprocal(ssq[:, 0:n], sums[:, 0:n]),
                lambda v: v.tensor_mul(ratio[:, 0:n], tmp16[:, 0:n], ssq[:, 0:n]),
            ])
        b.wait("scalar", "sV", v_var)
        if with_ratio:
            sop(lambda s, n=n: s.activation(ratio[:, 0:n], ratio[:, 0:n], AF.Sqrt))
        s_r = sop_inc(lambda s, n=n: s.activation(rstd[:, 0:n], rstd[:, 0:n], AF.Sqrt))
        b.wait("vector", "sS", s_r)
        if with_ratio:
            b.chain("vector", [
                lambda v, n=n: v.tensor_mul(ratio[:, 0:n], ratio[:, 0:n],
                                            rstd[:, 0:n])])
        return s_r

    def layernorm(chunks, src, dst, dump_sq, dump_sum, v_pre, t_pre, keep=False,
                  with_ratio=False):
        s_r = ln_stats(chunks, src, dump_sq, dump_sum, v_pre, t_pre, keep=keep,
                       with_ratio=with_ratio)
        b.s_r_last = s_r
        return b.vbatch([
            (lambda v, a=src(tc), d=dst(tc), i=i:
             v.tensor_scalar(d, a, mu[:, i:i + 1], rstd[:, i:i + 1],
                             ALU.subtract, ALU.mult))
            for i, tc in enumerate(chunks)])

    s_r1 = ln_stats(
        list(range(NTC)),
        src=lambda tc: resid[:, tc * D:(tc + 1) * D],
        dump_sq=gg[:, 0:D], dump_sum=gg[:, D:2 * D],
        v_pre=h1_done, t_pre=None, keep=True, with_ratio=True)
    b.s_r_last = s_r1
    v_zsc = b.chain("vector", [
        lambda v: v.tensor_copy(mu1[:, 0:NTC], mu[:, 0:NTC]),
        lambda v: v.tensor_scalar_mul(r2x[:, 0:NTC], rstd[:, 0:NTC], 2.0),
    ])
    b.v_zsc = v_zsc

    NU = 2 * NE * NJJ
    NW = 2 * NE * 2

    def unit_src(n):
        e = (n // NJJ) % NE
        jj = n % NJJ
        return wgu_d[e, jj, :, :, :, :].rearrange("a c p n -> p a c n")

    def wd_src(m):
        e = (m // 2) % NE
        p = m % 2
        return wd_d[e, p, :, :, :].rearrange("c p n -> p c n")

    gu_land = {}
    wd_land = {}

    def load_gu(n):
        k = n % 3
        b.wpair("gpsimd", b.buf_free.get(("gub", k)))
        gu_land[n] = gdma(
            f"gub{k}", gub[k][:, :].rearrange("p (a c n) -> p a c n", a=2, c=NFC),
            unit_src(n))

    def load_wd(m):
        b.wpair("gpsimd", b.buf_free.get(("wdb",)))
        wd_land[m] = gdma("wdb0", reshape3(wdb[0][:, :], HIDSLOTS), wd_src(m))

    load_gu(0)
    load_gu(1)
    load_gu(2)
    load_wd(0)

    blkA = nc.Block()
    blkA_g = blkA.__enter__()
    b.emit_into(blkA_g)
    blkA.__exit__(None, None, None)

    for c in reversed(actxs):
        c.__exit__(None, None, None)

    bctxs = []
    xnT = sb("xnT", [128, NFC * TH], BF16, bctxs)
    xe = sb("xe", [128, NFC * TH], BF16, bctxs)
    hid = sb("hid", [128, 2 * HIDSLOTS * TH], BF16, bctxs)
    acc = sb("acc", [128, NTH * D], F32, bctxs)
    z2 = sb("z2", [128, D], F32, bctxs)
    srep = sb("srep", [128, TH], BF16, bctxs)
    sg = [sb("sg0", [128, TH], BF16, bctxs)]
    logits = sb("logits", [128, NTH * NE], F32, bctxs)
    probs = sb("probs", [128, NTH * NE], F32, bctxs)
    mk = sb("mk", [128, NTH * NE], F32, bctxs)
    gates = mk
    gatesT = sb("gatesT", [NE, TH], F32, bctxs)
    rst8 = sums
    rst8b = ssq

    unit_ctr = 0
    wd_ctr = 0
    out_dma = [None, None]
    hid_ready = {}
    last_down_mil = None
    prep = {}

    def emit_srep_xe(e, banks, gT_done, xnT_done, keep=False):
        b.wait("tensor", "sV", gT_done)
        b.wpair("tensor", v_hcon)
        for slc in range(NSL):
            b.wpair("tensor", b.bank_free.get(banks[slc]))
            mm(bank(banks[slc], 0, SL), sel[:, e * 128:(e + 1) * 128],
               gatesT[:, slc * SL:(slc + 1) * SL], start=True, stop=True,
               inc=(slc == NSL - 1))
        tmil = b.cnt["sT"]
        b.wait("vector", "sT", tmil)
        b.wpair("vector", b.buf_free.get(("srep",)))
        v_srep = b.vbatch([
            (lambda v, slc=slc, bk=banks[slc]: v.tensor_copy(
                srep[:, slc * SL:(slc + 1) * SL], bank(bk, 0, SL)))
            for slc in range(NSL)])
        for slc in range(NSL):
            b.bank_free[banks[slc]] = ("sV", v_srep)
        b.wpair("vector", b.buf_free.get(("xe",)))
        b.wait("vector", "sS", xnT_done)
        b.wait("vector", "sV", v_srep)
        c0x = b.cnt["sV"]
        v_xe = b.chain("vector", [
            (lambda v, fc=fc: v.tensor_mul(
                xe[:, fc * TH:(fc + 1) * TH], xnT[:, fc * TH:(fc + 1) * TH],
                srep[:, 0:TH]))
            for fc in range(NFC)])
        if keep:
            keepalive([("sV", v) for v in range(c0x + 2, v_xe + 1, 2)])
        b.buf_free[("srep",)] = ("sV", v_xe)
        return v_xe

    def make_h1_prep_pieces():
        assert NTH == 8
        st = {}
        hc1 = NTH

        def merge(*fs):
            def f():
                for g in fs:
                    g()
            return f

        def z_piece(k):
            def f():
                tc = hc1 + k
                b.wpair("vector", b.buf_free.get(("z", 0)))
                st[("z", k)] = vop_inc(lambda v, tc=tc: v.tensor_scalar(
                    z2[:, 0:D], resid[:, tc * D:(tc + 1) * D],
                    mu1[:, tc:tc + 1], ratio[:, tc:tc + 1],
                    ALU.subtract, ALU.mult))
            return f

        def t_piece(k):
            def f():
                b.wait("tensor", "sV", st[("z", k)])
                b.wpair("tensor", b.bank_free.get(2))
                b.wpair("tensor", b.bank_free.get(3))
                tmil = None
                for fc in range(NFC):
                    bb = 2 + fc // 4
                    off = (fc % 4) * 128
                    if fc < NFC - 1:
                        b.op("tensor", lambda t, fc=fc, bb=bb, off=off:
                             t.transpose(bank(bb, off, 128),
                                         z2[:, fc * 128:(fc + 1) * 128],
                                         idn[:, :]))
                    else:
                        tmil = b.op_inc("tensor", lambda t, fc=fc, bb=bb, off=off:
                                        t.transpose(bank(bb, off, 128),
                                                    z2[:, fc * 128:(fc + 1) * 128],
                                                    idn[:, :]))
                b.buf_free[("z", 0)] = ("sT", tmil)
                b.wait("scalar", "sT", tmil)
                if k == 0:
                    b.wait("scalar", "sV", v_xe)
                ev = None
                for fc in range(NFC):
                    bb = 2 + fc // 4
                    off = (fc % 4) * 128
                    if fc < NFC - 1:
                        sop(lambda s, fc=fc, bb=bb, off=off: s.activation(
                            xnT[:, fc * TH + k * 128: fc * TH + (k + 1) * 128],
                            bank(bb, off, 128), AF.Identity,
                            bias=bet[:, fc:fc + 1], scale=gam[:, fc:fc + 1]))
                    else:
                        ev = b.op_inc("scalar", lambda s, fc=fc, bb=bb, off=off:
                                      s.activation(
                            xnT[:, fc * TH + k * 128: fc * TH + (k + 1) * 128],
                            bank(bb, off, 128), AF.Identity,
                            bias=bet[:, fc:fc + 1], scale=gam[:, fc:fc + 1]))
                b.bank_free[2] = ("sS", ev)
                b.bank_free[3] = ("sS", ev)
                st[("evac", k)] = ev
            return f

        def r_piece(ks, fin):
            def f():
                b.wpair("tensor", v_wcon)
                b.wait("tensor", "sS", st[("evac", max(ks))])
                b.wpair("tensor", b.bank_free.get(2))
                for k in ks:
                    for fc in range(NFC):
                        mm(bank(2, k * NE, NE),
                           xnT[:, fc * TH + k * 128: fc * TH + (k + 1) * 128],
                           rw[:, fc * NE:(fc + 1) * NE],
                           start=(fc == 0), stop=(fc == NFC - 1),
                           inc=(fin and k == ks[-1] and fc == NFC - 1))
                if fin:
                    b.wait("vector", "sT", b.cnt["sT"])
                    st["v_lg"] = vop_inc(lambda v: v.tensor_copy(
                        logits[:, 0:NTH * NE], bank(2, 0, NTH * NE)))
                    b.bank_free[2] = ("sV", st["v_lg"])
            return f

        def smax1():
            lg3 = reshape3(logits[:, 0:NTH * NE], NTH)

            def bc(stat):
                s3 = stat[:, 0:NTH].rearrange("p (c o) -> p c o", o=1)
                _, out = bass.broadcast_tensor_aps(lg3, s3)
                return out

            b.wait("vector", "sV", st["v_lg"])
            v_sub = b.chain("vector", [
                lambda v: v.tensor_scalar(logits[:, 0:NTH * NE],
                                          logits[:, 0:NTH * NE],
                                          float(CLAMP), float(-CLAMP),
                                          ALU.min, ALU.max),
                lambda v: v.tensor_reduce(
                    rst8[:, 0:NTH].rearrange("p (c o) -> p c o", o=1),
                    lg3, AX, ALU.max),
                lambda v: v.tensor_sub(lg3, lg3, bc(rst8)),
            ])
            b.wait("scalar", "sV", v_sub)
            st["s_exp"] = sop_inc(lambda s: s.activation(
                logits[:, 0:NTH * NE], logits[:, 0:NTH * NE], AF.Exp))

        def smax2():
            lg3 = reshape3(logits[:, 0:NTH * NE], NTH)
            pr3 = reshape3(probs[:, 0:NTH * NE], NTH)
            mk3 = reshape3(mk[:, 0:NTH * NE], NTH)

            def bc(stat):
                s3 = stat[:, 0:NTH].rearrange("p (c o) -> p c o", o=1)
                _, out = bass.broadcast_tensor_aps(lg3, s3)
                return out

            b.wait("vector", "sS", st["s_exp"])
            st["v_gt"] = b.chain("vector", [
                lambda v: v.tensor_reduce(
                    rst8[:, 0:NTH].rearrange("p (c o) -> p c o", o=1),
                    lg3, AX, ALU.add),
                lambda v: v.tensor_scalar_add(rst8[:, 0:NTH], rst8[:, 0:NTH],
                                              float(ROUTER_EPS)),
                lambda v: v.reciprocal(rst8b[:, 0:NTH], rst8[:, 0:NTH]),
                lambda v: v.tensor_mul(pr3, lg3, bc(rst8b)),
                lambda v: v.tensor_scalar(probs[:, 0:NTH * NE],
                                          probs[:, 0:NTH * NE],
                                          float(ROUTER_EPS), 1.0,
                                          ALU.max, ALU.min),
                lambda v: v.tensor_reduce(
                    rst8[:, 0:NTH].rearrange("p (c o) -> p c o", o=1),
                    pr3, AX, ALU.max),
                lambda v: v.tensor_tensor(mk3, pr3, bc(rst8), ALU.is_ge),
                lambda v: v.scalar_tensor_tensor(mk[:, 0:NTH * NE],
                                                 mk[:, 0:NTH * NE],
                                                 -1e30, probs[:, 0:NTH * NE],
                                                 ALU.mult, ALU.add),
                lambda v: v.tensor_reduce(
                    rst8b[:, 0:NTH].rearrange("p (c o) -> p c o", o=1),
                    mk3, AX, ALU.max),
                lambda v: v.tensor_tensor(mk3, pr3, bc(rst8b), ALU.is_ge),
                lambda v: v.tensor_mul(gates[:, 0:NTH * NE],
                                       probs[:, 0:NTH * NE],
                                       mk[:, 0:NTH * NE]),
            ])

        def gts():
            b.wait("tensor", "sV", st["v_gt"])
            b.wpair("tensor", b.bank_free.get(2))
            b.wpair("tensor", b.bank_free.get(3))
            tmil = None
            for i in range(NTH):
                bb = 2 + i // 4
                off = (i % 4) * 128
                if i < NTH - 1:
                    b.op("tensor", lambda t, i=i, bb=bb, off=off:
                         t.transpose(PS[0:NE, bb * 512 + off:bb * 512 + off + 128],
                                     gates[:, i * NE:(i + 1) * NE], idn[:, :]))
                else:
                    tmil = b.op_inc(
                        "tensor", lambda t, i=i, bb=bb, off=off:
                        t.transpose(PS[0:NE, bb * 512 + off:bb * 512 + off + 128],
                                    gates[:, i * NE:(i + 1) * NE], idn[:, :]))
            b.wait("vector", "sT", tmil)
            gT_done = b.vbatch([
                (lambda v, bb=bb, w=min(512, NTH * 128 - (bb - 2) * 512):
                 v.tensor_copy(gatesT[:, (bb - 2) * 512:(bb - 2) * 512 + w],
                               PS[0:NE, bb * 512:bb * 512 + w]))
                for bb in range(2, 2 + (NTH * 128 + 511) // 512)])
            b.bank_free[2] = ("sV", gT_done)
            b.bank_free[3] = ("sV", gT_done)
            prep["gT_done"] = gT_done

        def srx():
            prep["xnT_done"] = st[("evac", NTH - 1)]
            b.buf_free[("xe",)] = ("sT", prep["gu15"])
            prep["v_xe"] = emit_srep_xe(0, (2, 3), prep["gT_done"],
                                        prep["xnT_done"])

        P = [lambda: None for _ in range(3 * NTH)]
        P[0] = z_piece(0)
        for k in range(NTH):
            P[1 + 2 * k] = (merge(t_piece(k), z_piece(k + 1))
                            if k + 1 < NTH else t_piece(k))
        P[16] = r_piece((0, 1, 2, 3), fin=False)
        P[17] = r_piece((4, 5, 6, 7), fin=True)
        P[18] = smax1
        P[19] = smax2
        P[20] = gts
        P[21] = srx
        return P

    def inline_prep(h):
        hc0 = h * NTH
        b.wpair("scalar", v_hcon)
        b.wpair("vector", v_hcon)
        b.wait("vector", "sV", b.v_zsc)
        b.wait("vector", "sS", b.s_r_last)

        if True:
          xnT_done = None
          for i in range(NTH):
            tc = hc0 + i
            zs = 0
            b.wpair("vector", b.buf_free.get(("z", zs)))
            v_z = vop_inc(lambda v, tc=tc, zs=zs:
                          v.tensor_scalar(z2[:, zs * D:(zs + 1) * D],
                                          resid[:, tc * D:(tc + 1) * D],
                                          mu1[:, tc:tc + 1], ratio[:, tc:tc + 1],
                                          ALU.subtract, ALU.mult))
            keepalive([("sV", v_z)])
            b.wpair("tensor", v_hcon)
            bs = (i % 2) * 2
            b.wpair("tensor", b.bank_free.get(bs))
            b.wpair("tensor", b.bank_free.get(bs + 1))
            for fc in range(NFC):
                bb = bs + fc // 4
                off = (fc % 4) * 128
                if fc < NFC - 1:
                    b.op("tensor", lambda t, zs=zs, fc=fc, bb=bb, off=off:
                         t.transpose(bank(bb, off, 128),
                                     z2[:, zs * D + fc * 128: zs * D + (fc + 1) * 128],
                                     idn[:, :]))
                else:
                    tmil = b.op_inc("tensor", lambda t, zs=zs, fc=fc, bb=bb, off=off:
                                    t.transpose(bank(bb, off, 128),
                                                z2[:, zs * D + fc * 128: zs * D + (fc + 1) * 128],
                                                idn[:, :]))
            b.buf_free[("z", zs)] = ("sT", tmil)
            b.wait("scalar", "sT", tmil)
            b.wpair("scalar", v_hcon)
            for fc in range(NFC):
                bb = bs + fc // 4
                off = (fc % 4) * 128
                if fc < NFC - 1:
                    sop(lambda s, fc=fc, bb=bb, off=off, i=i: s.activation(
                        xnT[:, fc * TH + i * 128: fc * TH + (i + 1) * 128],
                        bank(bb, off, 128), AF.Identity,
                        bias=bet[:, fc:fc + 1], scale=gam[:, fc:fc + 1]))
                else:
                    xnT_done = b.op_inc("scalar", lambda s, fc=fc, bb=bb, off=off, i=i:
                                        s.activation(
                        xnT[:, fc * TH + i * 128: fc * TH + (i + 1) * 128],
                        bank(bb, off, 128), AF.Identity,
                        bias=bet[:, fc:fc + 1], scale=gam[:, fc:fc + 1]))
            b.bank_free[bs] = ("sS", xnT_done)
            b.bank_free[bs + 1] = ("sS", xnT_done)
            b.wait("tensor", "sS", xnT_done)
            b.wpair("tensor", v_wcon)
            b.wpair("tensor", b.bank_free.get(4))
            for fc in range(NFC):
                mm(bank(4, 0, NE), xnT[:, fc * TH + i * 128: fc * TH + (i + 1) * 128],
                   rw[:, fc * NE:(fc + 1) * NE],
                   start=(fc == 0), stop=(fc == NFC - 1), inc=(fc == NFC - 1))
            tmil = b.cnt["sT"]
            b.wait("vector", "sT", tmil)
            v_lg = vop_inc(lambda v, i=i: v.tensor_copy(
                logits[:, i * NE:(i + 1) * NE], bank(4, 0, NE)))
            b.bank_free[4] = ("sV", v_lg)
            keepalive([("sS", xnT_done), ("sV", v_lg)])


          lg3 = reshape3(logits[:, 0:NTH * NE], NTH)
          pr3 = reshape3(probs[:, 0:NTH * NE], NTH)
          mk3 = reshape3(mk[:, 0:NTH * NE], NTH)

          def bc(stat):
            s3 = stat[:, 0:NTH].rearrange("p (c o) -> p c o", o=1)
            _, out = bass.broadcast_tensor_aps(lg3, s3)
            return out

          b.wait("vector", "sV", v_lg)
          c0r = b.cnt["sV"]
          v_sub = b.chain("vector", [
            lambda v: v.tensor_scalar(logits[:, 0:NTH * NE], logits[:, 0:NTH * NE],
                                      float(CLAMP), float(-CLAMP), ALU.min, ALU.max),
            lambda v: v.tensor_reduce(
                rst8[:, 0:NTH].rearrange("p (c o) -> p c o", o=1), lg3, AX, ALU.max),
            lambda v: v.tensor_sub(lg3, lg3, bc(rst8)),
          ])
          b.wait("scalar", "sV", v_sub)
          s_exp = sop_inc(lambda s: s.activation(logits[:, 0:NTH * NE],
                                                 logits[:, 0:NTH * NE], AF.Exp))
          b.wait("vector", "sS", s_exp)
          v_gt = b.chain("vector", [
            lambda v: v.tensor_reduce(
                rst8[:, 0:NTH].rearrange("p (c o) -> p c o", o=1), lg3, AX, ALU.add),
            lambda v: v.tensor_scalar_add(rst8[:, 0:NTH], rst8[:, 0:NTH],
                                          float(ROUTER_EPS)),
            lambda v: v.reciprocal(rst8b[:, 0:NTH], rst8[:, 0:NTH]),
            lambda v: v.tensor_mul(pr3, lg3, bc(rst8b)),
            lambda v: v.tensor_scalar(probs[:, 0:NTH * NE], probs[:, 0:NTH * NE],
                                      float(ROUTER_EPS), 1.0, ALU.max, ALU.min),
            lambda v: v.tensor_reduce(
                rst8[:, 0:NTH].rearrange("p (c o) -> p c o", o=1), pr3, AX, ALU.max),
            lambda v: v.tensor_tensor(mk3, pr3, bc(rst8), ALU.is_ge),
            lambda v: v.scalar_tensor_tensor(mk[:, 0:NTH * NE], mk[:, 0:NTH * NE],
                                             -1e30, probs[:, 0:NTH * NE],
                                             ALU.mult, ALU.add),
            lambda v: v.tensor_reduce(
                rst8b[:, 0:NTH].rearrange("p (c o) -> p c o", o=1), mk3, AX, ALU.max),
            lambda v: v.tensor_tensor(mk3, pr3, bc(rst8b), ALU.is_ge),
            lambda v: v.tensor_mul(gates[:, 0:NTH * NE],
                                   probs[:, 0:NTH * NE], mk[:, 0:NTH * NE]),
          ])
          keepalive([("sV", v) for v in range(c0r + 2, v_gt + 1, 3)])

          b.wait("tensor", "sV", v_gt)
          b.wpair("tensor", b.bank_free.get(4))
          b.wpair("tensor", b.bank_free.get(5))
          tmil = None
          for i in range(NTH):
            bb = 4 + i // 4
            off = (i % 4) * 128
            if i < NTH - 1:
                b.op("tensor", lambda t, i=i, bb=bb, off=off:
                     t.transpose(PS[0:NE, bb * 512 + off:bb * 512 + off + 128],
                                 gates[:, i * NE:(i + 1) * NE], idn[:, :]))
            else:
                tmil = b.op_inc("tensor", lambda t, i=i, bb=bb, off=off:
                                t.transpose(PS[0:NE, bb * 512 + off:bb * 512 + off + 128],
                                            gates[:, i * NE:(i + 1) * NE], idn[:, :]))
          b.wait("vector", "sT", tmil)
          b.wait("vector", "sV", v_gt)
          gT_done = b.vbatch([
            (lambda v, bb=bb, w=min(512, NTH * 128 - (bb - 4) * 512):
             v.tensor_copy(gatesT[:, (bb - 4) * 512:(bb - 4) * 512 + w],
                           PS[0:NE, bb * 512:bb * 512 + w]))
            for bb in range(4, 4 + (NTH * 128 + 511) // 512)])
          b.bank_free[4] = ("sV", gT_done)
          b.bank_free[5] = ("sV", gT_done)

        return xnT_done, gT_done


    def gu_pass(p):
        nonlocal unit_ctr
        hset = (p % 2) * HIDSLOTS
        for j2 in range(HIDSLOTS):
            uk = unit_ctr % 3
            b.wpair("tensor", gu_land[unit_ctr])
            b.wpair("tensor", b.buf_free.get(("hid", p % 2, j2)))
            gb = 4 if j2 % 2 == 0 else 0
            ub = 6 if j2 % 2 == 0 else 2
            mils = {}
            for a, bs in ((0, gb), (1, ub)):
                for slc in range(NSL):
                    b.wpair("tensor", b.bank_free.get(bs + slc))
                for fc in range(NFC):
                    for slc in range(NSL):
                        mm(bank(bs + slc, 0, SL),
                           gub[uk][:, a * D + fc * 128:a * D + (fc + 1) * 128],
                           xe[:, fc * TH + slc * SL: fc * TH + (slc + 1) * SL],
                           start=(fc == 0), stop=(fc == NFC - 1),
                           inc=(fc == NFC - 1 and slc == NSL - 1))
                mils[a] = b.cnt["sT"]
            b.buf_free[("gub", uk)] = ("sT", mils[1])
            if unit_ctr + 3 < NU:
                load_gu(unit_ctr + 3)
            sgk = sg[0]
            b.wait("scalar", "sT", mils[0])
            b.wpair("scalar", b.buf_free.get(("sg",)))
            for slc in range(NSL):
                if slc < NSL - 1:
                    sop(lambda s, slc=slc, sgk=sgk, gb=gb: s.activation(
                        sgk[:, slc * SL:(slc + 1) * SL], bank(gb + slc, 0, SL),
                        AF.Silu))
                else:
                    s_silu = b.op_inc("scalar", lambda s, slc=slc, sgk=sgk, gb=gb:
                                      s.activation(sgk[:, slc * SL:(slc + 1) * SL],
                                                   bank(gb + slc, 0, SL), AF.Silu))
            for slc in range(NSL):
                b.bank_free[gb + slc] = ("sS", s_silu)
            b.wait("vector", "sT", mils[1])
            b.wait("vector", "sS", s_silu)
            v_hid = vop_inc(lambda v, c=(hset + j2) * TH, sgk=sgk, ub=ub:
                            v.tensor_mul(hid[:, c:c + TH], sgk[:, 0:TH],
                                         PS[:, ub * 512: ub * 512 + TH]))
            hid_ready[(p % 2, j2)] = v_hid
            b.buf_free[("sg",)] = ("sV", v_hid)
            for slc in range(NSL):
                b.bank_free[ub + slc] = ("sV", v_hid)
            unit_ctr += 1
        return b.cnt["sT"]

    def down_pass(p, first, srep_fn=None, weave=None):
        nonlocal wd_ctr, acc_ev, last_down_mil
        hset = (p % 2) * HIDSLOTS
        b.wpair("tensor", wd_land[wd_ctr])
        b.wpair("vector", b.buf_free.get(("accw",)))
        pairs = (4, 6, 0)
        srep_at = 0 if weave else min(2, NTH - 1)
        for i in range(NTH):
            if i == srep_at and srep_fn is not None:
                srep_fn()
            if weave:
                weave.pop(0)()
            bs = pairs[i % 3]
            b.wpair("tensor", b.bank_free.get(bs))
            b.wpair("tensor", b.bank_free.get(bs + 1))
            for ffc in range(HIDSLOTS):
                b.wait("tensor", "sV", hid_ready[(p % 2, ffc)])
                for oh in range(2):
                    mm(bank(bs + oh),
                       hid[:, (hset + ffc) * TH + i * 128:
                           (hset + ffc) * TH + (i + 1) * 128],
                       wdb[0][:, ffc * D + oh * 512: ffc * D + oh * 512 + 512],
                       start=(ffc == 0), stop=(ffc == HIDSLOTS - 1),
                       inc=(ffc == HIDSLOTS - 1 and oh == 1))
            tmil = b.cnt["sT"]
            b.wait("vector", "sT", tmil)
            if p == 2 * NE and first and i == 0:
                b.wpair("vector", out_dma[0])
            if first:
                acc_ev = b.vbatch([
                    (lambda v, c=i * D + oh * 512, bb=bs + oh:
                     v.tensor_copy(acc[:, c:c + 512], bank(bb)))
                    for oh in range(2)])
            else:
                acc_ev = b.vbatch([
                    (lambda v, c=i * D + oh * 512, bb=bs + oh:
                     v.tensor_add(acc[:, c:c + 512], bank(bb),
                                  acc[:, c:c + 512]))
                    for oh in range(2)])
            b.bank_free[bs] = ("sV", acc_ev)
            b.bank_free[bs + 1] = ("sV", acc_ev)
        b.buf_free[("accw",)] = ("sV", acc_ev)
        last_down_mil = b.cnt["sT"]
        for j2 in range(HIDSLOTS):
            b.buf_free[("hid", p % 2, j2)] = ("sT", last_down_mil)
        b.buf_free[("wdb",)] = ("sT", last_down_mil)
        if wd_ctr + 1 < NW:
            load_wd(wd_ctr + 1)
        wd_ctr += 1

    def epilogue(h):
        hc0 = h * NTH
        b.wait("vector", "sV", acc_ev)
        fns_h2 = []
        for i in range(NTH):
            tc = hc0 + i
            fns_h2.append(lambda v, tc=tc: v.tensor_scalar(
                resid[:, tc * D:(tc + 1) * D], resid[:, tc * D:(tc + 1) * D],
                mu1[:, tc:tc + 1], r2x[:, tc:tc + 1], ALU.subtract, ALU.mult))
            fns_h2.append(lambda v, tc=tc, i=i: v.tensor_add(
                resid[:, tc * D:(tc + 1) * D], resid[:, tc * D:(tc + 1) * D],
                acc[:, i * D:(i + 1) * D]))
        h2_done = b.chain("vector", fns_h2)
        ln3_done = layernorm(
            list(range(hc0, hc0 + NTH)),
            src=lambda tc: resid[:, tc * D:(tc + 1) * D],
            dst=lambda tc: acc[:, (tc - hc0) * D:(tc - hc0 + 1) * D],
            dump_sq=acc[:, 0:D], dump_sum=z2[:, 0:D],
            v_pre=h2_done, t_pre=last_down_mil, keep=False)
        b.wait("sync", "sV", ln3_done)
        out_dma[h] = sdma(f"out{h}",
                          y_d[hc0:hc0 + NTH, :, :].rearrange("c p n -> p c n"),
                          reshape3(acc[:, 0:NTH * D], NTH))

    WEAVE = (NTH == 8)
    xnT_done, gT_done = inline_prep(0)
    v_xe = emit_srep_xe(0, (2, 3), gT_done, xnT_done, keep=True)
    wv = make_h1_prep_pieces() if WEAVE else None
    acc_ev = None
    pending = None
    gu_last = None
    for p in range(4 * NE):
        e, hp = (p // 2) % NE, p % 2
        if p == 4 * NE - 1 and pending is not None:
            down_pass(pending, first=False, srep_fn=None)
            pending = None
        if p == 2 * NE:
            if WEAVE:
                xnT_done = prep["xnT_done"]
                gT_done = prep["gT_done"]
                v_xe = prep["v_xe"]
            else:
                b.buf_free[("xe",)] = ("sT", gu_last)
                xnT_done, gT_done = inline_prep(1)
                v_xe = emit_srep_xe(0, (2, 3), gT_done, xnT_done)
        if hp == 0:
            b.wait("tensor", "sV", v_xe)
        gu_last = gu_pass(p)
        if p == 2 * NE - 1:
            prep["gu15"] = gu_last
        srep_fn = None
        if hp == 1 and e + 1 < NE:
            b.buf_free[("xe",)] = ("sT", gu_last)

            def srep_fn(e=e):
                nonlocal v_xe
                v_xe = emit_srep_xe(e + 1, (2, 3), gT_done, xnT_done)
        if pending is not None:
            down_pass(pending, first=(pending % (2 * NE) == 0), srep_fn=srep_fn,
                      weave=(wv if wv is not None and 12 <= pending <= 14
                             else None))
            if pending == 2 * NE - 1:
                epilogue(0)
        pending = p
    down_pass(pending, first=False, srep_fn=None)
    if wv is not None:
        assert not wv, f"unconsumed weave pieces: {len(wv)}"
    epilogue(1)

    b.wpair("sync", out_dma[0])
    b.wpair("sync", out_dma[1])

    blkB = nc.Block()
    blkB_g = blkB.__enter__()
    b.emit_into(blkB_g)
    blkB.__exit__(None, None, None)

    for c in reversed(bctxs):
        c.__exit__(None, None, None)
    for c in reversed(ctxs):
        c.__exit__(None, None, None)
    return nc


def kernel(L0_attn_x, L0_attn_W_theta, L0_attn_W_phi, L0_attn_W_g,
           L0_ffn_ln_gamma, L0_ffn_ln_beta, L0_ffn_router_w,
           L0_ffn_gate_up_w, L0_ffn_down_w):
    x = np.ascontiguousarray(np.asarray(L0_attn_x, dtype=np.float32))
    B, S, _ = x.shape
    NCORES = 8
    CPB = NCORES // B
    T = S // CPB

    def wT_chunks(w):
        return np.ascontiguousarray(np.asarray(w, np.float32).T.reshape(NFC, 128, -1))

    wth_h = wT_chunks(L0_attn_W_theta)
    wph_h = wT_chunks(L0_attn_W_phi)
    wg_h = wT_chunks(L0_attn_W_g)
    rw_h = wT_chunks(L0_ffn_router_w)
    gam_h = np.ascontiguousarray(np.asarray(L0_ffn_ln_gamma, np.float32).reshape(NFC, 128).T)
    bet_h = np.ascontiguousarray(np.asarray(L0_ffn_ln_beta, np.float32).reshape(NFC, 128).T)
    wgu = np.asarray(L0_ffn_gate_up_w, np.float32)
    wgu_h = np.ascontiguousarray(
        wgu.reshape(NE, 2, NJJ, 128, D).transpose(0, 2, 1, 4, 3)
           .reshape(NE, NJJ, 2, NFC, 128, 128))
    wd = np.asarray(L0_ffn_down_w, np.float32)
    wd_h = np.ascontiguousarray(
        wd.transpose(0, 2, 1).reshape(NE, 2, HIDSLOTS, 128, D))
    sel_h = np.zeros((NE, NE * 128), np.float32)
    for e in range(NE):
        sel_h[e, e * 128:(e + 1) * 128] = 1.0
    idn_h = np.eye(128, dtype=np.float32)

    xT_batches = [np.ascontiguousarray(x[bb].T) for bb in range(B)]

    nc = build_program(T=T, TB=S, GRP=512)
    in_maps = []
    for c in range(NCORES):
        bb = c // CPB
        s0 = (c % CPB) * T
        xt = np.ascontiguousarray(x[bb, s0:s0 + T].reshape(T // 128, 128, D))
        xTb = xT_batches[bb]
        if USE_CC:
            xT_core = np.ascontiguousarray(xTb[:, s0:s0 + T].reshape(NFC, 128, T))
        else:
            xT_core = np.ascontiguousarray(
                np.concatenate([xTb[:, s0:s0 + T], xTb[:, :s0], xTb[:, s0 + T:]],
                               axis=1).reshape(NFC, 128, S))
        in_maps.append({
            "xtok": xt, "xT": xT_core,
            "wth": wth_h, "wph": wph_h, "wg": wg_h, "rw": rw_h,
            "gam": gam_h, "bet": bet_h,
            "wgu": wgu_h, "wd": wd_h, "sel": sel_h, "idn": idn_h,
        })
    res = run_bass_kernel_spmd(nc, in_maps, core_ids=list(range(NCORES)))
    outs = [np.asarray(res.results[c]["y"]).reshape(T, D) for c in range(NCORES)]
    full = np.concatenate(outs, axis=0).reshape(B, S, D)
    return full.astype(np.float32)
